# version 09
# baseline (speedup 1.0000x reference)
import contextlib
import numpy as np
import ml_dtypes

import concourse.bass as bass
import concourse.mybir as mybir
from concourse.bass_utils import run_bass_kernel_spmd

F32 = mybir.dt.float32
BF16 = mybir.dt.bfloat16
AF = mybir.ActivationFunctionType
ALU = mybir.AluOpType
AX = mybir.AxisListType

D = 1024
G = 8
NS = 16
GN = G * NS
HID = 4096
NH = HID // 128
NER, NES = 8, 2
EPS_LN = 1e-5
EPS_RMS = 1e-6
EPS_FFN = 1.1920929e-7
TT = 512
NCORES = 8
BATCH, SEQ = 4, 4096


class Rg:
    __slots__ = ("name", "w", "readers")

    def __init__(self, name):
        self.name = name
        self.w = None
        self.readers = {}


class Bld:
    CENG = ("v", "s", "t", "p", "y")
    NLANE = {"qg": 6, "qs": 6, "qa": 4}

    def __init__(self, nc, es):
        keys = ["v", "s", "t", "p"]
        for q, n in self.NLANE.items():
            keys += [f"{q}{i}" for i in range(n)]
        self.sem = {k: es.enter_context(nc.semaphore("sem_" + k)) for k in keys}
        self.cnt = dict.fromkeys(keys, 0)
        self.qidx = {q: 0 for q in self.NLANE}
        self.wm = {e: {} for e in self.CENG}
        self.plan = {e: [] for e in self.CENG}

    def _waits(self, issuer, reads, writes):
        need = {}
        for r in reads:
            if r.w is not None:
                k, v = r.w
                if v > need.get(k, 0):
                    need[k] = v
        for w in writes:
            if w.w is not None:
                k, v = w.w
                if v > need.get(k, 0):
                    need[k] = v
            for k, v in w.readers.items():
                if v > need.get(k, 0):
                    need[k] = v
        out = []
        wmi = self.wm[issuer]
        for k in sorted(need):
            v = need[k]
            if k == issuer and issuer == "t":
                continue
            if v > wmi.get(k, 0):
                out.append((k, v))
                wmi[k] = v
        return out

    def op(self, eng, emit, reads=(), writes=(), inc=1):
        waits = self._waits(eng, reads, writes)
        if inc:
            self.cnt[eng] += inc
            val = self.cnt[eng]
            self.plan[eng].append((waits, emit, eng, inc))
        else:
            val = self.cnt[eng] + 1
            self.plan[eng].append((waits, emit, None, 0))
        for r in reads:
            if r.readers.get(eng, 0) < val:
                r.readers[eng] = val
        for w in writes:
            w.w = (eng, val)
            w.readers = {}

    def dma(self, q, emit, reads=(), writes=()):
        issuer = {"qg": "p", "qs": "y", "qa": "s"}[q]
        lane = f"{q}{self.qidx[q] % self.NLANE[q]}"
        self.qidx[q] += 1
        waits = self._waits(issuer, reads, writes)
        wmi = self.wm[issuer]
        if self.cnt[lane] > wmi.get(lane, 0):
            waits.append((lane, self.cnt[lane]))
            wmi[lane] = self.cnt[lane]
        self.cnt[lane] += 16
        val = self.cnt[lane]
        self.plan[issuer].append((waits, emit, lane, 16))
        for r in reads:
            if r.readers.get(lane, 0) < val:
                r.readers[lane] = val
        for w in writes:
            w.w = (lane, val)
            w.readers = {}

    def final_wait(self, eng, q):
        for i in range(self.NLANE[q]):
            lane = f"{q}{i}"
            if self.cnt[lane]:
                self.plan[eng].append(
                    ([(lane, self.cnt[lane])], lambda e: e.nop(), None, 0))

    def finalize(self, block, boot_v=0):
        sems = self.sem
        for ename, attr in (("v", "vector"), ("s", "scalar"), ("t", "tensor"),
                            ("p", "gpsimd"), ("y", "sync")):
            items = self.plan[ename]
            pre = [] if (ename == "v" or not boot_v) else [("v", boot_v)]

            def body(eng, items=items, pre=pre):
                for (k, v) in pre:
                    eng.wait_ge(sems[k], v)
                for waits, emit, semk, inc in items:
                    for (k, v) in waits:
                        eng.wait_ge(sems[k], v)
                    ins = emit(eng)
                    if semk is not None:
                        ins.then_inc(sems[semk], inc)
            getattr(block, attr)(body)


def build_program(NT=4, NTP=4, upto=None):
    TC = NT * TT
    TP = NTP * TT
    NTG = TC // 128
    nc = bass.Bass("TRN2", target_bir_lowering=False)

    dr = {}

    def din(name, shape, dt=F32):
        dr[name] = nc.dram_tensor(name, list(shape), dt, kind="ExternalInput")

    din("x", (D, TC))
    din("xp", (D, TP))
    din("w1", (D, HID))
    din("w2", (HID, D))
    din("W1s", (NES, D, HID))
    din("W2s", (NES, HID, D))
    din("W1r", (NER, D, HID))
    din("W2r", (NER, HID, D))
    din("A_r", (128, GN))
    din("B_r", (128, GN))
    din("C_r", (128, GN))
    din("s1_r", (128, G))
    din("s2_r", (128, G))
    din("D_r", (128, G))
    din("gs_r", (128, G))
    din("gr_r", (128, G))
    din("b2a_r", (128, G))
    din("b1a_r", (128, NH))
    din("b1s_r", (128, NES * NH))
    din("b1r_r", (128, NER * NH))
    din("b2r", (NER, D))
    din("b2s0", (1, D))
    din("b2s1", (1, D))
    din("cent_r", (128, G * NER))
    din("IDENT", (128, 128))
    din("ONESBF", (128, 128), BF16)
    din("DTREP", (128, GN))
    din("DTRI", (128, GN))
    din("RJREV", (128, TP))
    din("WMTC", (16, 128))
    out_h = nc.dram_tensor("out", [TC, D], F32, kind="ExternalOutput")
    uhat_d = nc.dram_tensor("uhatd", [128, G * TC], BF16, kind="Internal")
    w2c_d = nc.dram_tensor("w2c", [128, (1 + NES + NER) * (HID // 512) * 4096],
                           BF16, kind="Internal")
    dbg_h = {}
    if upto is not None:
        for nm, shape in (("dbg_x", [128, G * TC]),
                          ("dbg_m", [128, G * TC // 2]),
                          ("dbg_wtok", [128, NTG * NER]),
                          ("dbg_smid", [128, GN]),
                          ("dbg_otm", [128, NTG * 1024])):
            dbg_h[nm] = nc.dram_tensor(nm, shape, F32, kind="ExternalOutput")

    es = contextlib.ExitStack()
    with es:
        def sbt(name, shape, dt=F32):
            return es.enter_context(nc.sbuf_tensor(name, list(shape), dt))

        XR = sbt("XR", (128, G * TC))
        BIGC = max(4 * G * TT, 8 * TC)
        BIG = sbt("BIG", (128, BIGC))
        MXSC = max(G * TC // 2, TP + 5120 + NTP * 128 + 2 * GN,
                   G * TT + 4096)
        MXS = sbt("MXS", (128, MXSC))
        SSb = sbt("SSb", (128, 4 * TT + 2 * GN + TT))
        STAT = sbt("STAT", (128, 5 * TT + 64))
        WBUF = sbt("WBUF", (128, 4096))
        COEF = sbt("COEF", (128, 5 * GN + 8 * G))
        BIA = sbt("BIA", (128, (1 + NES + NER) * NH))
        B2AUG = sbt("B2AUG", (16, D))
        WTOK = sbt("WTOK", (128, NTG * NER))
        WMT = sbt("WMT", (16, 128))
        IDE = sbt("IDE", (128, 128))
        ONE = sbt("ONE", (128, 128), BF16)

        DTRO = TP + 5120 + NTP * 128
        DTR = MXS[:, DTRO:DTRO + GN]
        DTI = MXS[:, DTRO + GN:DTRO + 2 * GN]
        SCT = STAT[0:8, 3 * TT:4 * TT]

        ps = [es.enter_context(nc.psum_tensor(f"ps{i}", [128, TT], F32))
              for i in range(8)]

        b = Bld(nc, es)
        block = es.enter_context(nc.Block())

        CONSTV = sbt("CONSTV", (128, 16))
        constv_rg = Rg("constv")
        for i, val in enumerate((0.0, 1.0, -0.5, -1.0, EPS_LN, EPS_RMS,
                                 EPS_FFN)):
            b.op("v", (lambda e, ap=CONSTV[:, i:i + 1], v=val:
                       e.memset(ap, v)), writes=[constv_rg])
            nc.const_aps.register(F32, float(val), CONSTV[:, i:i + 1])
        boot_v = b.cnt["v"]

        xr_rg = [Rg(f"xr{i}") for i in range(G * TC // TT)]
        big_rg = [Rg(f"big{i}") for i in range(BIGC // TT)]
        mxs_rg = [Rg(f"mxs{i}") for i in range(MXSC // TT + 1)]

        def rgs(lst, c0, c1):
            return [lst[i] for i in range(c0 // TT, (c1 + TT - 1) // TT)]

        ss_s = [Rg(f"ss_s{i}") for i in range(4)]
        ss_tmp = Rg("ss_tmp")
        ss_scol = [Rg(f"scol{gn}") for gn in range(GN)]
        ss_smid = Rg("smid")
        st_rg = [Rg(f"st{i}") for i in range(5)]
        st_mic = Rg("stmic")
        wb_w1 = [Rg("wbw1a"), Rg("wbw1b")]
        wb_w2 = [Rg("wbw2a"), Rg("wbw2b")]
        coef_rg = Rg("coef")
        bia_rg = Rg("bia")
        b2aug_rg = Rg("b2aug")
        wtok_rg = [Rg(f"wtok{tg}") for tg in range(NTG)]
        wmt_rg = Rg("wmt")
        sct_rg = Rg("sct")
        const_rg = Rg("const")
        ps_rg = [Rg(f"psr{i}") for i in range(8)]
        uhd_rg = Rg("uhatd")
        w2c_rg = [Rg(f"w2c{i}") for i in range(1 + NES + NER)]
        out_rg = Rg("outh")
        dbg_rg = Rg("dbg")

        def TSc(out, in0, s1, s2, op0, op1=ALU.bypass, acc=None):
            return lambda e: e.tensor_scalar(out, in0, s1, s2, op0, op1,
                                             accum_out=acc)

        def TTo(out, a, c, op):
            return lambda e: e.tensor_tensor(out, a, c, op)

        def STT(out, in0, s, in1, op0, op1, acc=None):
            return lambda e: e.scalar_tensor_tensor(out, in0, s, in1, op0, op1,
                                                    accum_out=acc)

        def CPY(out, in0):
            return lambda e: e.tensor_copy(out, in0)

        def ACT(out, in0, fn, bias=0.0, scale=1.0, acc=None):
            return lambda e: e.activation(out, in0, fn, bias=bias, scale=scale,
                                          accum_out=acc)

        def DMA(out, in0):
            return lambda e: e.dma_start(out=out, in_=in0)

        def MSET(ap, v):
            return lambda e: e.memset(ap, v)

        def bcg(row_ap, n):
            return row_ap.unsqueeze(1).broadcast_to([128, G, n])

        def v3(ap, n):
            return ap.rearrange("p (dg t) -> p dg t", t=n)

        cLogdA = COEF[:, 0:GN]
        cdB = COEF[:, GN:2 * GN]
        cC = COEF[:, 2 * GN:3 * GN]
        cdA = COEF[:, 3 * GN:4 * GN]
        cCB = COEF[:, 4 * GN:5 * GN]
        V8 = 5 * GN

        def cv8(i):
            return COEF[:, V8 + i * G:V8 + (i + 1) * G]

        def srw(i, n=TT):
            return STAT[:, i * TT:i * TT + n]
        MIC = 5 * TT

        def mic(i, n=8):
            return STAT[:, MIC + i * 8:MIC + i * 8 + n]

        b.op("v", MSET(B2AUG[0:16, :], 0.0), writes=[b2aug_rg])
        for dst, src, rg in [
                (IDE[:, :], dr["IDENT"][:, :], const_rg),
                (WMT[0:16, :], dr["WMTC"][:, :], wmt_rg),
                (ONE[:, :], dr["ONESBF"][:, :], const_rg),
                (DTR, dr["DTREP"][:, :], const_rg),
                (DTI, dr["DTRI"][:, :], const_rg),
                (cLogdA, dr["A_r"][:, :], coef_rg),
                (cdB, dr["B_r"][:, :], coef_rg),
                (cC, dr["C_r"][:, :], coef_rg),
                (cv8(0), dr["s1_r"][:, :], coef_rg),
                (cv8(1), dr["s2_r"][:, :], coef_rg),
                (cv8(2), dr["D_r"][:, :], coef_rg),
                (cv8(3), dr["gs_r"][:, :], coef_rg),
                (cv8(4), dr["gr_r"][:, :], coef_rg),
                (cv8(5), dr["b2a_r"][:, :], coef_rg),
                (BIA[:, 0:NH], dr["b1a_r"][:, :], bia_rg),
                (BIA[:, NH:(1 + NES) * NH], dr["b1s_r"][:, :], bia_rg),
                (BIA[:, (1 + NES) * NH:(1 + NES + NER) * NH],
                 dr["b1r_r"][:, :], bia_rg),
                (B2AUG[0:8, :], dr["b2r"][:, :], b2aug_rg),
                (B2AUG[8:9, :], dr["b2s0"][:, :], b2aug_rg),
                (B2AUG[9:10, :], dr["b2s1"][:, :], b2aug_rg)]:
            b.dma("qs", DMA(dst, src), writes=[rg])

        xv = dr["x"][:, :].rearrange("(dg p) t -> p dg t", p=128)
        b.dma("qs", DMA(v3(XR[:, :], TC), xv), writes=list(xr_rg))

        b.op("s", ACT(cdA, cLogdA, AF.Exp), reads=[coef_rg], writes=[coef_rg])
        b.op("v", STT(cLogdA, cdA, -1.0, DTR, ALU.mult, ALU.mult),
             reads=[coef_rg, const_rg], writes=[coef_rg])
        b.op("s", ACT(cdA, cLogdA, AF.Exp), reads=[coef_rg], writes=[coef_rg])
        b.op("v", TSc(cCB, cdA, -1.0, 1.0, ALU.mult, ALU.add),
             reads=[coef_rg], writes=[coef_rg])
        b.op("v", TTo(cdB, cdB, cCB, ALU.mult), reads=[coef_rg],
             writes=[coef_rg])
        b.op("v", STT(cCB, cLogdA, -1.0, DTI, ALU.mult, ALU.mult),
             reads=[coef_rg, const_rg], writes=[coef_rg])
        b.op("v", lambda e: e.reciprocal(cCB, cCB), reads=[coef_rg],
             writes=[coef_rg])
        b.op("v", TTo(cdB, cdB, cCB, ALU.mult), reads=[coef_rg],
             writes=[coef_rg])
        b.op("v", TTo(cCB, cC, cdB, ALU.mult), reads=[coef_rg],
             writes=[coef_rg])
        for g in range(G):
            b.op("v", TSc(cCB[:, g * NS:(g + 1) * NS],
                          cCB[:, g * NS:(g + 1) * NS],
                          cv8(0)[:, g:g + 1], None, ALU.mult),
                 reads=[coef_rg], writes=[coef_rg])
        b.op("v", TTo(cv8(6), cv8(2), cv8(0), ALU.mult),
             reads=[coef_rg], writes=[coef_rg])


        def stats(xbf_ap, xbf_rgs, eps2=None):
            def e_s1(e):
                last = None
                for dg in range(G):
                    last = e.matmul(ps[6][:, 0:TT], ONE[:, :],
                                    xbf_ap[:, dg * TT:(dg + 1) * TT],
                                    start=(dg == 0), stop=(dg == G - 1))
                return last
            b.op("t", e_s1, reads=xbf_rgs + [const_rg], writes=[ps_rg[6]])
            b.op("s", ACT(xbf_ap, xbf_ap, AF.Square), reads=xbf_rgs,
                 writes=xbf_rgs)

            def e_s2(e):
                last = None
                for dg in range(G):
                    last = e.matmul(ps[7][:, 0:TT], ONE[:, :],
                                    xbf_ap[:, dg * TT:(dg + 1) * TT],
                                    start=(dg == 0), stop=(dg == G - 1))
                return last
            b.op("t", e_s2, reads=xbf_rgs + [const_rg], writes=[ps_rg[7]])
            var, rstd, mu, ntm = srw(0), srw(1), srw(2), srw(3)
            rv, rr, rm, rn = st_rg[0], st_rg[1], st_rg[2], st_rg[3]
            b.op("v", TSc(mu, ps[6][:, 0:TT], 1.0 / D, None, ALU.mult),
                 reads=[ps_rg[6]], writes=[rm])
            b.op("v", TTo(var, mu, mu, ALU.mult), reads=[rm], writes=[rv])
            b.op("v", STT(var, ps[7][:, 0:TT], 1.0 / D, var, ALU.mult,
                          ALU.subtract), reads=[ps_rg[7], rv], writes=[rv])
            if eps2 is not None:
                inv2 = srw(4)
                ri = st_rg[4]
                b.op("v", TSc(inv2, var, EPS_LN, None, ALU.add),
                     reads=[rv], writes=[ri])
                b.op("v", lambda e, o=inv2, i=inv2: e.reciprocal(o, i),
                     reads=[ri], writes=[ri])
                b.op("v", TTo(inv2, inv2, var, ALU.mult),
                     reads=[ri, rv], writes=[ri])
                b.op("s", ACT(inv2, inv2, AF.Ln, bias=eps2),
                     reads=[ri], writes=[ri])
                b.op("s", ACT(inv2, inv2, AF.Exp, scale=-0.5),
                     reads=[ri], writes=[ri])
            b.op("v", TSc(var, var, EPS_LN, None, ALU.add),
                 reads=[rv], writes=[rv])
            b.op("s", ACT(rstd, var, AF.Ln), reads=[rv], writes=[rr])
            b.op("s", ACT(rstd, rstd, AF.Exp, scale=-0.5),
                 reads=[rr], writes=[rr])
            b.op("v", TTo(ntm, rstd, rstd, ALU.mult), reads=[rr], writes=[rn])
            b.op("v", TTo(ntm, ntm, var, ALU.mult), reads=[rn, rv],
                 writes=[rn])
            b.op("v", TSc(ntm, ntm, -0.5, 1.5, ALU.mult, ALU.add),
                 reads=[rn], writes=[rn])
            b.op("v", TTo(rstd, rstd, ntm, ALU.mult),
                 reads=[rr, rn], writes=[rr])
            return var, rstd, mu

        def stats_sq_only(xbf_ap, xbf_rgs, eps):
            def e_s2(e):
                last = None
                for dg in range(G):
                    last = e.matmul(ps[7][:, 0:TT], ONE[:, :],
                                    xbf_ap[:, dg * TT:(dg + 1) * TT],
                                    start=(dg == 0), stop=(dg == G - 1))
                return last
            b.op("t", e_s2, reads=xbf_rgs + [const_rg], writes=[ps_rg[7]])
            inv = srw(4)
            ri = st_rg[4]
            b.op("v", TSc(inv, ps[7][:, 0:TT], 1.0 / D, None, ALU.mult),
                 reads=[ps_rg[7]], writes=[ri])
            b.op("s", ACT(inv, inv, AF.Ln, bias=eps), reads=[ri], writes=[ri])
            b.op("s", ACT(inv, inv, AF.Exp, scale=-0.5),
                 reads=[ri], writes=[ri])
            return inv, ri

        def w2_quad_cols(m_idx, quad):
            return (m_idx * (HID // 512) + quad) * 4096

        def w2cache_prep(m_idx, src_ap_fn):
            stage, st_rgs = w2slot[0]
            stage_f32 = WBUF[:, 0:2048]
            bld, bl_rgs = w2slot[1]
            for quad in range(HID // 512):
                for hq in range(2):
                    b.dma("qs", DMA(
                        stage_f32.rearrange("p (h2 d) -> p h2 d", d=D),
                        src_ap_fn(quad * 512 + hq * 256)), writes=st_rgs)
                    b.op("s", ACT(bld[:, hq * 2048:(hq + 1) * 2048],
                                  stage_f32, AF.Copy),
                         reads=st_rgs, writes=[bl_rgs[hq]])
                c0 = w2_quad_cols(m_idx, quad)
                b.dma("qs", DMA(w2c_d[:, c0:c0 + 4096], bld),
                      reads=bl_rgs, writes=[w2c_rg[m_idx]])

        w1slot = [(STAT[:, 0:1024].bitcast(BF16), [st_rg[0], st_rg[1]]),
                  (STAT[:, 1024:2048].bitcast(BF16), [st_rg[2], st_rg[3]])]
        w2slot = [(WBUF[:, 0:2048].bitcast(BF16), [wb_w2[0], wb_w2[1]]),
                  (WBUF[:, 2048:4096].bitcast(BF16), [wb_w1[0], wb_w1[1]])]


        smid_ap = SSb[:, 4 * TT + GN:4 * TT + 2 * GN]
        b.op("v", MSET(smid_ap, 0.0), writes=[ss_smid])
        rjr = MXS[:, 0:TP]
        rj_rg = rgs(mxs_rg, 0, TP)
        b.dma("qs", DMA(rjr, dr["RJREV"][:, :]), writes=rj_rg)
        xpv = dr["xp"][:, :].rearrange("(dg p) t -> p dg t", p=128)

        xbfo = TP
        expwo = TP + 2048
        NEXPW = 4
        dumpo = expwo + NEXPW * TT
        smpo = dumpo + TT
        xbf_rg_p1 = rgs(mxs_rg, xbfo, xbfo + 2048)
        expw_rg = [rgs(mxs_rg, expwo + i * TT, expwo + (i + 1) * TT)[0]
                   for i in range(NEXPW)]
        dump_rg = rgs(mxs_rg, dumpo, dumpo + TT)
        smp_rg = [rgs(mxs_rg, smpo + i * 128, smpo + (i + 1) * 128)[0]
                  for i in range(NTP)]

        for pt in range(NTP):
            sl = pt % 2
            xpo = G * TT * (2 + sl)
            xp_ap = BIG[:, xpo:xpo + G * TT]
            xp_rgs = rgs(big_rg, xpo, xpo + G * TT)
            b.dma("qs", DMA(v3(xp_ap, TT), xpv[:, :, pt * TT:(pt + 1) * TT]),
                  writes=xp_rgs)
            xbf = MXS[:, xbfo:xbfo + 2048].bitcast(BF16)
            b.dma("qg", DMA(xbf.rearrange("p (dg t) -> p dg t", t=TT),
                            xpv[:, :, pt * TT:(pt + 1) * TT]),
                  writes=xbf_rg_p1)
            var, rstd, mu = stats(xbf, xbf_rg_p1, eps2=EPS_RMS)
            inv2 = srw(4)
            comb, rc = srw(3), st_rg[3]
            mucomb, rmc = srw(0), st_rg[0]
            b.op("v", TTo(comb, rstd, inv2, ALU.mult),
                 reads=[st_rg[1], st_rg[4]], writes=[rc])
            b.op("v", TTo(mucomb, mu, comb, ALU.mult),
                 reads=[st_rg[2], rc], writes=[rmc])
            upo = G * TT * sl
            up_ap = BIG[:, upo:upo + G * TT]
            up_rgs = rgs(big_rg, upo, upo + G * TT)
            b.op("v", TTo(v3(up_ap, TT), v3(xp_ap, TT), bcg(comb, TT),
                          ALU.mult), reads=xp_rgs + [rc], writes=up_rgs)
            b.op("v", TTo(v3(up_ap, TT), v3(up_ap, TT), bcg(mucomb, TT),
                          ALU.subtract), reads=up_rgs + [rmc], writes=up_rgs)
            smp_ap = MXS[:, smpo + pt * 128:smpo + (pt + 1) * 128]
            for gn in range(GN):
                g = gn // NS
                esl = gn % NEXPW
                ew = MXS[:, expwo + esl * TT:expwo + (esl + 1) * TT]
                b.op("s", ACT(ew, rjr[:, pt * TT:(pt + 1) * TT], AF.Exp,
                              scale=cLogdA[:, gn:gn + 1]),
                     reads=rj_rg + [coef_rg], writes=[expw_rg[esl]])
                b.op("v", STT(MXS[:, dumpo:dumpo + TT], ew, 1.0,
                              up_ap[:, g * TT:(g + 1) * TT], ALU.mult,
                              ALU.mult, acc=smp_ap[:, gn:gn + 1]),
                     reads=[expw_rg[esl], up_rgs[g]],
                     writes=dump_rg + [smp_rg[pt]])
        for pt in range(NTP):
            smp_ap = MXS[:, smpo + pt * 128:smpo + (pt + 1) * 128]
            b.op("v", TTo(smid_ap, smid_ap, smp_ap, ALU.add),
                 reads=[smp_rg[pt], ss_smid], writes=[ss_smid])

        prep_jobs = [(0, lambda hr: dr["w2"][hr:hr + 256, :]
                      .rearrange("(h2 p) d -> p h2 d", p=128))]
        for e_i in range(NES):
            prep_jobs.append((1 + e_i, lambda hr, e_i=e_i:
                              dr["W2s"][e_i, hr:hr + 256, :]
                              .rearrange("(h2 p) d -> p h2 d", p=128)))
        for e_i in range(NER):
            prep_jobs.append((3 + e_i, lambda hr, e_i=e_i:
                              dr["W2r"][e_i, hr:hr + 256, :]
                              .rearrange("(h2 p) d -> p h2 d", p=128)))

        USO, YO, XBO = 0, G * TT, 2 * G * TT
        UBO = 3 * G * TT
        us_ap = BIG[:, USO:USO + G * TT]
        y_ap = BIG[:, YO:YO + G * TT]
        us_rgs = rgs(big_rg, USO, USO + G * TT)
        y_rgs = rgs(big_rg, YO, YO + G * TT)
        m_bf = MXS[:, 0:G * TC // 2].bitcast(BF16)

        def m_rgs(dg, t):
            return rgs(mxs_rg, dg * TC // 2 + t * TT // 2,
                       dg * TC // 2 + (t + 1) * TT // 2)

        for t in range(NT):
            sl = t % 2
            xbf = BIG[:, XBO + sl * 2048:XBO + (sl + 1) * 2048].bitcast(BF16)
            xbf_rgs = rgs(big_rg, XBO + sl * 2048, XBO + (sl + 1) * 2048)
            b.dma("qg", DMA(xbf.rearrange("p (dg t) -> p dg t", t=TT),
                            xv[:, :, t * TT:(t + 1) * TT]), writes=xbf_rgs)
            var, rstd, mu = stats(xbf, xbf_rgs, eps2=EPS_RMS)
            inv2, ri2 = srw(4), st_rg[4]
            comb, rc = srw(3), st_rg[3]
            b.op("v", TTo(comb, rstd, inv2, ALU.mult),
                 reads=[st_rg[1], ri2], writes=[rc])
            mucomb, rmc = srw(1), st_rg[1]
            b.op("v", TTo(mucomb, mu, comb, ALU.mult),
                 reads=[st_rg[2], rc], writes=[rmc])
            xt = v3(XR[:, :], TC)[:, :, t * TT:(t + 1) * TT]
            xt_rgs = [xr_rg[(dg * TC + t * TT) // TT] for dg in range(G)]
            b.op("v", TTo(v3(us_ap, TT), xt, bcg(comb, TT), ALU.mult),
                 reads=xt_rgs + [rc], writes=us_rgs)
            b.op("v", TTo(v3(us_ap, TT), v3(us_ap, TT), bcg(mucomb, TT),
                          ALU.subtract), reads=us_rgs + [rmc], writes=us_rgs)
            rc2, rrc2 = srw(2), st_rg[2]
            b.op("v", lambda e, o=rc2, i=inv2: e.reciprocal(o, i),
                 reads=[ri2], writes=[rrc2])
            b.op("v", MSET(y_ap, 0.0), writes=y_rgs)
            yp_ap = BIG[:, UBO:UBO + G * TT]
            yp_rgs = rgs(big_rg, UBO, UBO + G * TT)
            b.op("p", lambda e: e.memset(yp_ap, 0.0), writes=yp_rgs)
            tmp_p = SSb[:, 4 * TT + 2 * GN:5 * TT + 2 * GN]
            for gn in range(GN):
                g = gn // NS
                ssl = gn % 4
                s_ap = SSb[:, ssl * TT:(ssl + 1) * TT]
                if t == 0:
                    init = smid_ap[:, gn:gn + 1]
                    init_rg = ss_smid
                else:
                    init = SSb[:, 4 * TT + gn:4 * TT + gn + 1]
                    init_rg = ss_scol[gn]
                b.op("v", lambda e, o=s_ap, gn=gn, g=g, ini=init:
                     e.tensor_tensor_scan(
                         o, cdA[:, gn:gn + 1].broadcast_to([128, TT]),
                         us_ap[:, g * TT:(g + 1) * TT], ini,
                         ALU.mult, ALU.add),
                     reads=[coef_rg, init_rg, us_rgs[g]],
                     writes=[ss_s[ssl]])
                if gn % 2 == 0:
                    b.op("v", STT(y_ap[:, g * TT:(g + 1) * TT], s_ap,
                                  cCB[:, gn:gn + 1],
                                  y_ap[:, g * TT:(g + 1) * TT],
                                  ALU.mult, ALU.add),
                         reads=[ss_s[ssl], coef_rg, y_rgs[g]],
                         writes=[y_rgs[g]])
                else:
                    b.op("p", TTo(tmp_p, s_ap,
                                  cCB[:, gn:gn + 1].broadcast_to([128, TT]),
                                  ALU.mult),
                         reads=[ss_s[ssl], coef_rg], writes=[ss_tmp])
                    b.op("p", TTo(yp_ap[:, g * TT:(g + 1) * TT],
                                  yp_ap[:, g * TT:(g + 1) * TT], tmp_p,
                                  ALU.add),
                         reads=[ss_tmp, yp_rgs[g]], writes=[yp_rgs[g]])
                if t < NT - 1:
                    b.op("p", CPY(SSb[:, 4 * TT + gn:4 * TT + gn + 1],
                                  s_ap[:, TT - 1:TT]),
                         reads=[ss_s[ssl]], writes=[ss_scol[gn]])
            b.op("v", TTo(v3(y_ap, TT), v3(y_ap, TT), v3(yp_ap, TT), ALU.add),
                 reads=y_rgs + yp_rgs, writes=y_rgs)
            for dg in range(G):
                b.op("v", STT(y_ap[:, dg * TT:(dg + 1) * TT],
                              us_ap[:, dg * TT:(dg + 1) * TT],
                              cv8(6)[:, dg:dg + 1],
                              y_ap[:, dg * TT:(dg + 1) * TT],
                              ALU.mult, ALU.add),
                     reads=[us_rgs[dg], coef_rg, y_rgs[dg]],
                     writes=[y_rgs[dg]])
            b.op("v", TTo(v3(us_ap, TT), v3(us_ap, TT), bcg(rc2, TT),
                          ALU.mult), reads=us_rgs + [rrc2], writes=us_rgs)
            b.op("v", TTo(v3(y_ap, TT), v3(y_ap, TT), v3(us_ap, TT), ALU.add),
                 reads=y_rgs + us_rgs, writes=y_rgs)
            xbf2 = BIG[:, XBO + sl * 2048:XBO + (sl + 1) * 2048].bitcast(BF16)
            b.op("s", ACT(xbf2, y_ap, AF.Square), reads=y_rgs,
                 writes=xbf_rgs)
            inv3, ri3 = stats_sq_only(xbf2, xbf_rgs, EPS_RMS)
            b.op("v", TTo(v3(us_ap, TT), v3(y_ap, TT), bcg(inv3, TT),
                          ALU.mult), reads=y_rgs + [ri3], writes=us_rgs)
            for dg in range(G):
                b.op("s", ACT(m_bf[:, dg * TC + t * TT:dg * TC + (t + 1) * TT],
                              us_ap[:, dg * TT:(dg + 1) * TT], AF.Copy,
                              scale=cv8(1)[:, dg:dg + 1]),
                     reads=[us_rgs[dg], coef_rg], writes=m_rgs(dg, t))
            b.op("v", TTo(xt, xt, v3(y_ap, TT), ALU.add),
                 reads=xt_rgs + y_rgs, writes=xt_rgs)
            nj = 3 if t < NT - 1 else len(prep_jobs)
            for _ in range(min(nj, len(prep_jobs))):
                mi, fn = prep_jobs.pop(0)
                w2cache_prep(mi, fn)

        def dump(which):
            if "x" in which:
                b.dma("qs", DMA(dbg_h["dbg_x"][:, :], XR[:, :]),
                      reads=list(xr_rg), writes=[dbg_rg])
            if "m" in which:
                b.dma("qs", DMA(dbg_h["dbg_m"][:, :],
                                MXS[:, 0:G * TC // 2]),
                      reads=list(mxs_rg[:G * TC // 2 // TT]), writes=[dbg_rg])
            if "smid" in which:
                b.dma("qs", DMA(dbg_h["dbg_smid"][:, :], smid_ap),
                      reads=[ss_smid], writes=[dbg_rg])
            if "wtok" in which:
                b.dma("qs", DMA(dbg_h["dbg_wtok"][:, :], WTOK[:, :]),
                      reads=list(wtok_rg), writes=[dbg_rg])
            if "otm" in which:
                b.dma("qs", DMA(dbg_h["dbg_otm"][:, :], BIG[:, 0:NTG * 1024]),
                      reads=rgs(big_rg, 0, NTG * 1024), writes=[dbg_rg])
            b.final_wait("y", "qs")

        if upto == "P2":
            dump(("x", "m", "smid"))
            b.finalize(block, boot_v)
            return nc

        NQ = 4
        QH = HID // NQ
        bigbf = BIG[:, :].bitcast(BF16)
        m_all = rgs(mxs_rg, 0, G * TC // 2)

        def h3_ap(slab, hsub, tt):
            c = slab * (G * TC) + hsub * TC + tt * TT
            return bigbf[:, c:c + TT], rgs(big_rg, c // 2, (c + TT) // 2)

        for q in range(NQ):
            slab = q % 2
            for hc in range(QH // 256):
                wsl = (q * (QH // 256) + hc) % 2
                w1c, w1rgs = w1slot[wsl]
                h0 = q * QH + hc * 256
                src = dr["w1"][:, h0:h0 + 256].rearrange(
                    "(dg p) h -> p dg h", p=128)
                b.dma("qg", DMA(w1c.rearrange("p (dg h) -> p dg h", h=256),
                                src), writes=w1rgs)
                for hs in range(2):
                    hsub = hc * 2 + hs
                    for tt in range(NT):
                        pp = ps[(hsub * NT + tt) % 3]
                        pr = ps_rg[(hsub * NT + tt) % 3]

                        def e_mm1(e, pp=pp, w1c=w1c, hs=hs, tt=tt):
                            last = None
                            for dg in range(G):
                                last = e.matmul(
                                    pp[:, 0:TT],
                                    w1c[:, dg * 256 + hs * 128:
                                        dg * 256 + (hs + 1) * 128],
                                    m_bf[:, dg * TC + tt * TT:
                                         dg * TC + (tt + 1) * TT],
                                    start=(dg == 0), stop=(dg == G - 1))
                            return last
                        b.op("t", e_mm1, reads=w1rgs + m_all,
                             writes=[pr])
                        hap, h_rgs = h3_ap(slab, hsub, tt)
                        b.op("s", ACT(hap, pp[:, 0:TT], AF.Gelu,
                                      bias=BIA[:, q * 8 + hsub:
                                               q * 8 + hsub + 1]),
                             reads=[pr, bia_rg], writes=h_rgs)
            for tt in range(NT):
                for hsub in range(8):
                    psl = (hsub // 4) % 2
                    if hsub % 4 == 0:
                        w2p, w2rgs = w2slot[psl]
                        c0 = w2_quad_cols(0, q * 2 + hsub // 4)
                        b.dma("qs", DMA(w2p, w2c_d[:, c0:c0 + 4096]),
                              reads=[w2c_rg[0]], writes=w2rgs)
                    w2p, w2rgs = w2slot[psl]
                    w2t = w2p[:, (hsub % 4) * D:(hsub % 4 + 1) * D]
                    hap, h_rgs = h3_ap(slab, hsub, tt)

                    def e_mm2(e, w2t=w2t, hap=hap, hsub=hsub):
                        last = None
                        for dmg in range(G):
                            last = e.matmul(ps[dmg][:, 0:TT],
                                            w2t[:, dmg * 128:(dmg + 1) * 128],
                                            hap, start=(hsub == 0),
                                            stop=(hsub == 7))
                        return last
                    b.op("t", e_mm2, reads=w2rgs + h_rgs,
                         writes=[ps_rg[dmg] for dmg in range(G)])
                for dmg in range(G):
                    xap = XR[:, dmg * TC + tt * TT:dmg * TC + (tt + 1) * TT]
                    xrg = xr_rg[(dmg * TC + tt * TT) // TT]
                    if q < NQ - 1:
                        b.op("v", TTo(xap, xap, ps[dmg][:, 0:TT], ALU.add),
                             reads=[ps_rg[dmg], xrg], writes=[xrg])
                    else:
                        b.op("v", STT(xap, ps[dmg][:, 0:TT],
                                      cv8(5)[:, dmg:dmg + 1], xap,
                                      ALU.add, ALU.add),
                             reads=[ps_rg[dmg], coef_rg, xrg], writes=[xrg])

        if upto == "P3":
            dump(("x", "m"))
            b.finalize(block, boot_v)
            return nc

        uo = 0
        u_ap = MXS[:, uo:uo + G * TT]
        u_rgs = rgs(mxs_rg, uo, uo + G * TT)
        uh_o = G * TT
        uhs_rg = [rgs(mxs_rg, uh_o, uh_o + 2048),
                  rgs(mxs_rg, uh_o + 2048, uh_o + 4096)]
        cent_sb = WBUF[:, 0:64]
        b.dma("qs", DMA(cent_sb, dr["cent_r"][:, :]), writes=[wb_w2[0]])
        xbf4 = WBUF[:, 64:64 + 2048].bitcast(BF16)
        xbf4_rgs = [wb_w2[0], wb_w2[1], wb_w1[0], wb_w1[1]]

        for t in range(NT):
            xt = v3(XR[:, :], TC)[:, :, t * TT:(t + 1) * TT]
            xt_rgs = [xr_rg[(dg * TC + t * TT) // TT] for dg in range(G)]
            b.op("v", CPY(xbf4.rearrange("p (dg t) -> p dg t", t=TT), xt),
                 reads=xt_rgs, writes=xbf4_rgs)
            var, rstd, mu = stats(xbf4, xbf4_rgs, eps2=EPS_FFN)
            inv4, ri4 = srw(4), st_rg[4]
            mucomb, rmc = srw(3), st_rg[3]
            b.op("v", TTo(mucomb, mu, rstd, ALU.mult),
                 reads=[st_rg[2], st_rg[1]], writes=[rmc])
            b.op("v", TTo(v3(u_ap, TT), xt, bcg(rstd, TT), ALU.mult),
                 reads=xt_rgs + [st_rg[1]], writes=u_rgs)
            b.op("v", TTo(v3(u_ap, TT), v3(u_ap, TT), bcg(mucomb, TT),
                          ALU.subtract), reads=u_rgs + [rmc], writes=u_rgs)

            def e_sc(e):
                last = None
                for dg in range(G):
                    last = e.matmul(ps[3][0:8, 0:TT],
                                    cent_sb[:, dg * 8:(dg + 1) * 8],
                                    u_ap[:, dg * TT:(dg + 1) * TT],
                                    start=(dg == 0), stop=(dg == G - 1))
                return last
            b.op("t", e_sc, reads=u_rgs + [wb_w2[0]], writes=[ps_rg[3]])
            b.op("v", CPY(SCT[0:8, 0:TT], ps[3][0:8, 0:TT]),
                 reads=[ps_rg[3]], writes=[st_rg[3]])
            for tgl in range(TT // 128):
                tg = t * (TT // 128) + tgl
                b.op("t", lambda e, tgl=tgl: e.transpose(
                    ps[3][:, 0:8], SCT[0:8, tgl * 128:(tgl + 1) * 128],
                    IDE[0:8, 0:8]), reads=[st_rg[3], const_rg],
                    writes=[ps_rg[3]])
                lg = mic(0)
                b.op("v", CPY(lg, ps[3][:, 0:8]), reads=[ps_rg[3]],
                     writes=[st_mic])
                m1 = mic(4, 1)
                b.op("v", lambda e, o=m1, i=lg: e.reduce_max(o, i, AX.X),
                     reads=[st_mic], writes=[st_mic])
                mk1 = mic(1)
                b.op("v", TSc(mk1, lg, m1, None, ALU.is_ge),
                     reads=[st_mic], writes=[st_mic])
                t2 = mic(2)
                b.op("v", STT(t2, mk1, -1e30, lg, ALU.mult, ALU.add),
                     reads=[st_mic], writes=[st_mic])
                m2 = mic(5, 1)
                b.op("v", lambda e, o=m2, i=t2: e.reduce_max(o, i, AX.X),
                     reads=[st_mic], writes=[st_mic])
                mk = mic(1)
                b.op("v", TSc(mk, lg, m2, None, ALU.is_ge),
                     reads=[st_mic], writes=[st_mic])
                sg = mic(3)
                b.op("s", ACT(sg, lg, AF.Exp, scale=-1.0),
                     reads=[st_mic], writes=[st_mic])
                b.op("v", TSc(sg, sg, 1.0, None, ALU.add),
                     reads=[st_mic], writes=[st_mic])
                b.op("v", lambda e, o=sg, i=sg: e.reciprocal(o, i),
                     reads=[st_mic], writes=[st_mic])
                dn = mic(6, 1)
                b.op("v", lambda e, o=dn, i=sg: e.reduce_sum(o, i, AX.X),
                     reads=[st_mic], writes=[st_mic])
                wv = mic(2)
                b.op("v", TTo(wv, sg, mk, ALU.mult), reads=[st_mic],
                     writes=[st_mic])
                b.op("v", lambda e, o=dn, i=dn: e.reciprocal(o, i),
                     reads=[st_mic], writes=[st_mic])
                b.op("v", TSc(WTOK[:, tg * 8:(tg + 1) * 8], wv, dn, None,
                              ALU.mult), reads=[st_mic],
                     writes=[wtok_rg[tg]])
            usl = t % 2
            uh_ap = MXS[:, uh_o + usl * 2048:uh_o + (usl + 1) * 2048]
            uh_bf = uh_ap.bitcast(BF16)
            b.op("v", TTo(uh_bf.rearrange("p (dg t) -> p dg t", t=TT),
                          v3(u_ap, TT), bcg(inv4, TT), ALU.mult),
                 reads=u_rgs + [ri4], writes=uhs_rg[usl])
            b.dma("qg", DMA(
                dr_uhat_view(uhat_d, t, TC), uh_bf.rearrange(
                    "p (dg t) -> p dg t", t=TT)),
                reads=uhs_rg[usl], writes=[uhd_rg])
            b.op("v", TTo(v3(u_ap, TT), v3(u_ap, TT), xt, ALU.add),
                 reads=u_rgs + xt_rgs, writes=u_rgs)
            for tgl in range(TT // 128):
                tg = t * (TT // 128) + tgl
                for dg in range(G):
                    pt_ = ps[4 + dg // 4]
                    b.op("t", lambda e, dg=dg, tgl=tgl, pt_=pt_: e.transpose(
                        pt_[:, (dg % 4) * 128:(dg % 4 + 1) * 128],
                        u_ap[:, dg * TT + tgl * 128:dg * TT + (tgl + 1) * 128],
                        IDE[:, :]),
                        reads=u_rgs + [const_rg],
                        writes=[ps_rg[4 + dg // 4]],
                        inc=(1 if dg % 4 == 3 else 0))
                for hh in range(2):
                    b.op("v", CPY(BIG[:, tg * 1024 + hh * 512:
                                       tg * 1024 + (hh + 1) * 512],
                                  ps[4 + hh][:, 0:TT]),
                         reads=[ps_rg[4 + hh]],
                         writes=rgs(big_rg, tg * 1024 + hh * 512,
                                    tg * 1024 + (hh + 1) * 512))

        if upto == "P4":
            dump(("wtok", "otm"))
            b.finalize(block, boot_v)
            return nc

        xn_bf = MXS[:, 0:G * TC // 2].bitcast(BF16)
        xn_all = rgs(mxs_rg, 0, G * TC // 2)
        xrbf = XR[:, :].bitcast(BF16)

        n_slab5 = max(1, (G * TC) // 8192)

        def h5(slab, hsub):
            c = slab * 16384 + hsub * TT
            return (xrbf[:, c:c + TT],
                    rgs(xr_rg, c // 2, (c + TT) // 2))

        def load_xn(gvec_idx):
            for half in range(2):
                b.dma("qg", DMA(
                    xn_bf[:, half * (G * TC // 2):(half + 1) * (G * TC // 2)],
                    uhat_d[:, half * (G * TC // 2):(half + 1) * (G * TC // 2)]),
                    reads=[uhd_rg], writes=xn_all)
            for dg in range(G):
                b.op("v", TSc(xn_bf[:, dg * TC:(dg + 1) * TC],
                              xn_bf[:, dg * TC:(dg + 1) * TC],
                              cv8(gvec_idx)[:, dg:dg + 1], None, ALU.mult),
                     reads=xn_all + [coef_rg], writes=xn_all)

        def expert(eidx, w1_src, w2m, b1col, wcol):
            PSZ = min(2, n_slab5)
            for pq in range(NT // PSZ):
                tqs = [pq * PSZ + i for i in range(PSZ)]
                for hc in range(HID // 256):
                    wsl = hc % 2
                    w1c, w1rgs = w1slot[wsl]
                    b.dma("qg", DMA(
                        w1c.rearrange("p (dg h) -> p dg h", h=256),
                        w1_src(hc * 256, 256)), writes=w1rgs)
                    for hs in range(2):
                        hsub = hc * 2 + hs
                        for tq in tqs:
                            pidx = hsub * PSZ + (tq % PSZ)
                            pp = ps[pidx % 3]
                            pr = ps_rg[pidx % 3]

                            def e_mm1(e, pp=pp, w1c=w1c, hs=hs, tq=tq):
                                last = None
                                for dg in range(G):
                                    last = e.matmul(
                                        pp[:, 0:TT],
                                        w1c[:, dg * 256 + hs * 128:
                                            dg * 256 + (hs + 1) * 128],
                                        xn_bf[:, dg * TC + tq * TT:
                                              dg * TC + (tq + 1) * TT],
                                        start=(dg == 0), stop=(dg == G - 1))
                                return last
                            b.op("t", e_mm1, reads=w1rgs + xn_all,
                                 writes=[pr])
                            hap, h_rgs = h5(tq % n_slab5, hsub)
                            b.op("s", ACT(hap, pp[:, 0:TT], AF.Gelu,
                                          bias=BIA[:, b1col + hsub:
                                                   b1col + hsub + 1]),
                                 reads=[pr, bia_rg], writes=h_rgs)
                for tq in tqs:
                    mm2t_body(tq, tq % n_slab5, w2m, wcol)

        def mm2t_body(tq, slab, w2m, wcol):
            for hsub in range(NH):
                psl = (hsub // 4) % 2
                if hsub % 4 == 0:
                    w2p, w2rgs = w2slot[psl]
                    c0 = w2_quad_cols(w2m, hsub // 4)
                    b.dma("qs", DMA(w2p, w2c_d[:, c0:c0 + 4096]),
                          reads=[w2c_rg[w2m]], writes=w2rgs)
                w2p, w2rgs = w2slot[psl]
                w2t = w2p[:, (hsub % 4) * D:(hsub % 4 + 1) * D]
                hap, h_rgs = h5(slab, hsub)

                def e_mm2(e, w2t=w2t, hap=hap, hsub=hsub):
                    last = None
                    for tgl in range(4):
                        for dmh in range(2):
                            last = e.matmul(
                                ps[tgl * 2 + dmh][:, 0:TT],
                                hap[:, tgl * 128:(tgl + 1) * 128],
                                w2t[:, dmh * 512:(dmh + 1) * 512],
                                start=(hsub == 0), stop=(hsub == NH - 1))
                    return last
                b.op("t", e_mm2, reads=w2rgs + h_rgs,
                     writes=[ps_rg[i] for i in range(8)])
            for tgl in range(4):
                tg = tq * 4 + tgl
                for dmh in range(2):
                    oap = BIG[:, tg * 1024 + dmh * 512:
                              tg * 1024 + (dmh + 1) * 512]
                    orgs = rgs(big_rg, tg * 1024 + dmh * 512,
                               tg * 1024 + (dmh + 1) * 512)
                    scl = (1.0 if wcol is None else
                           WTOK[:, tg * 8 + wcol:tg * 8 + wcol + 1])
                    rds = [ps_rg[tgl * 2 + dmh]] + orgs
                    if wcol is not None:
                        rds.append(wtok_rg[tg])
                    b.op("v", STT(oap, ps[tgl * 2 + dmh][:, 0:TT], scl,
                                  oap, ALU.mult, ALU.add),
                         reads=rds, writes=orgs)

        load_xn(3)
        for e_i in range(NES):
            expert(e_i,
                   lambda h0, n, e_i=e_i: dr["W1s"][e_i, :, h0:h0 + n]
                   .rearrange("(dg p) h -> p dg h", p=128),
                   1 + e_i, NH + e_i * NH, None)
        load_xn(4)
        for e_i in range(NER):
            expert(e_i,
                   lambda h0, n, e_i=e_i: dr["W1r"][e_i, :, h0:h0 + n]
                   .rearrange("(dg p) h -> p dg h", p=128),
                   3 + e_i, (1 + NES) * NH + e_i * NH, e_i)

        for tg in range(NTG):
            b.op("t", lambda e, tg=tg: e.transpose(
                ps[3][:, 0:8]
                if False else ps[3][0:8, 0:128],
                WTOK[:, tg * 8:(tg + 1) * 8], IDE[:, :]),
                reads=[wtok_rg[tg], const_rg], writes=[ps_rg[3]])
            b.op("v", CPY(WMT[0:8, :], ps[3][0:8, 0:128]),
                 reads=[ps_rg[3]], writes=[wmt_rg])
            for dmh in range(2):
                b.op("t", lambda e, dmh=dmh: e.matmul(
                    ps[4 + dmh][:, 0:TT], WMT[0:16, :],
                    B2AUG[0:16, dmh * 512:(dmh + 1) * 512],
                    start=True, stop=True),
                    reads=[wmt_rg, b2aug_rg], writes=[ps_rg[4 + dmh]])
            for dmh in range(2):
                oap = BIG[:, tg * 1024 + dmh * 512:
                          tg * 1024 + (dmh + 1) * 512]
                orgs = rgs(big_rg, tg * 1024 + dmh * 512,
                           tg * 1024 + (dmh + 1) * 512)
                b.op("v", TTo(oap, oap, ps[4 + dmh][:, 0:TT], ALU.add),
                     reads=[ps_rg[4 + dmh]] + orgs, writes=orgs)
            b.dma("qa", DMA(out_h[tg * 128:(tg + 1) * 128, :],
                            BIG[:, tg * 1024:(tg + 1) * 1024]),
                  reads=rgs(big_rg, tg * 1024, (tg + 1) * 1024),
                  writes=[out_rg])
        b.final_wait("y", "qa")
        b.finalize(block, boot_v)
        return nc


def dr_uhat_view(uhat_d, t, TC):
    return uhat_d[:, :].rearrange("p (dg t) -> p dg t", t=TC)[
        :, :, t * TT:(t + 1) * TT]


def _prep_shared(inp):
    f32 = np.float32
    r = {}
    rs = lambda a: np.ascontiguousarray(a)
    A = np.asarray(inp["L0_attn_A_ssm"], f32)
    B = np.asarray(inp["L0_attn_B_ssm"], f32)
    C = np.asarray(inp["L0_attn_C_ssm"], f32)
    resh = lambda M: rs(M.reshape(G, 128, NS).transpose(1, 0, 2)
                        .reshape(128, GN))
    r["A_r"], r["B_r"], r["C_r"] = resh(A), resh(B), resh(C)
    vec8 = lambda v: rs(np.asarray(v, f32).reshape(G, 128).T)
    r["s1_r"] = vec8(inp["L0_attn_scale1"])
    r["s2_r"] = vec8(inp["L0_attn_scale2"])
    r["D_r"] = vec8(inp["L0_attn_D_ssm"])
    r["gs_r"] = vec8(inp["L0_ffn_g_shared"])
    r["gr_r"] = vec8(inp["L0_ffn_g_routed"])
    r["b2a_r"] = vec8(inp["L0_attn_b2"])
    r["b1a_r"] = rs(np.asarray(inp["L0_attn_b1"], f32).reshape(NH, 128).T)
    r["b1s_r"] = rs(np.asarray(inp["L0_ffn_b1_s"], f32)
                    .reshape(NES, NH, 128).transpose(2, 0, 1)
                    .reshape(128, NES * NH))
    r["b1r_r"] = rs(np.asarray(inp["L0_ffn_b1_r"], f32)
                    .reshape(NER, NH, 128).transpose(2, 0, 1)
                    .reshape(128, NER * NH))
    r["b2r"] = np.asarray(inp["L0_ffn_b2_r"], f32)
    r["b2s0"] = rs(np.asarray(inp["L0_ffn_b2_s"], f32)[0:1])
    r["b2s1"] = rs(np.asarray(inp["L0_ffn_b2_s"], f32)[1:2])
    r["cent_r"] = rs(np.asarray(inp["L0_ffn_centroids"], f32)
                     .reshape(G, 128, NER).transpose(1, 0, 2)
                     .reshape(128, G * NER))
    r["w1"] = np.asarray(inp["L0_attn_w1"], f32)
    r["w2"] = np.asarray(inp["L0_attn_w2"], f32)
    r["W1s"] = np.asarray(inp["L0_ffn_W1_s"], f32)
    r["W2s"] = np.asarray(inp["L0_ffn_W2_s"], f32)
    r["W1r"] = np.asarray(inp["L0_ffn_W1_r"], f32)
    r["W2r"] = np.asarray(inp["L0_ffn_W2_r"], f32)
    r["IDENT"] = np.eye(128, dtype=f32)
    r["ONESBF"] = np.ones((128, 128), dtype=ml_dtypes.bfloat16)
    import math as _m
    dt = np.exp(np.linspace(_m.log(0.001), _m.log(0.1), NS)).astype(f32)
    r["DTREP"] = rs(np.broadcast_to(np.tile(dt, G)[None, :], (128, GN)))
    r["DTRI"] = rs(np.broadcast_to(np.tile((1.0 / dt).astype(f32), G)[None, :],
                                   (128, GN)))
    return r



def kernel(**inputs):
    TC = 2048
    NT = TC // TT
    x = np.asarray(inputs["L0_attn_x"], np.float32)
    shared = _prep_shared(inputs)
    rj = np.ascontiguousarray(np.broadcast_to(
        (TC - 1.0 - np.arange(TC, dtype=np.float32))[None, :], (128, TC)))
    zeros_fm = np.zeros((D, TC), np.float32)

    nc = build_program(NT=NT, NTP=NT)
    in_maps = []
    for c in range(NCORES):
        bb, hf = c // 2, c % 2
        m = dict(shared)
        m["x"] = np.ascontiguousarray(
            x[bb, hf * TC:(hf + 1) * TC, :].T)
        m["xp"] = (np.ascontiguousarray(x[bb, 0:TC, :].T)
                   if hf else zeros_fm)
        m["RJREV"] = rj
        wmtc = np.zeros((16, 128), np.float32)
        wmtc[8:10] = 1.0
        m["WMTC"] = wmtc
        in_maps.append(m)
    res = run_bass_kernel_spmd(nc, in_maps, core_ids=list(range(NCORES)))
    outs = [res.results[c]["out"] for c in range(NCORES)]
    return np.concatenate(outs, axis=0).reshape(BATCH, SEQ, D)


# version 10
# speedup vs baseline: 1.1086x; 1.1086x over previous
import contextlib
import numpy as np
import ml_dtypes

import concourse.bass as bass
import concourse.mybir as mybir
from concourse.bass_utils import run_bass_kernel_spmd

F32 = mybir.dt.float32
BF16 = mybir.dt.bfloat16
AF = mybir.ActivationFunctionType
ALU = mybir.AluOpType
AX = mybir.AxisListType

D = 1024
G = 8
NS = 16
GN = G * NS
HID = 4096
NH = HID // 128
NER, NES = 8, 2
EPS_LN = 1e-5
EPS_RMS = 1e-6
EPS_FFN = 1.1920929e-7
TT = 512
NCORES = 8
BATCH, SEQ = 4, 4096


class Rg:
    __slots__ = ("name", "w", "readers")

    def __init__(self, name):
        self.name = name
        self.w = None
        self.readers = {}


class Bld:
    CENG = ("v", "s", "t", "p", "y")
    NLANE = {"qg": 6, "qs": 6, "qa": 4}

    def __init__(self, nc, es):
        keys = ["v", "s", "t", "p"]
        for q, n in self.NLANE.items():
            keys += [f"{q}{i}" for i in range(n)]
        self.sem = {k: es.enter_context(nc.semaphore("sem_" + k)) for k in keys}
        self.cnt = dict.fromkeys(keys, 0)
        self.qidx = {q: 0 for q in self.NLANE}
        self.wm = {e: {} for e in self.CENG}
        self.plan = {e: [] for e in self.CENG}

    def _waits(self, issuer, reads, writes):
        need = {}
        for r in reads:
            if r.w is not None:
                k, v = r.w
                if v > need.get(k, 0):
                    need[k] = v
        for w in writes:
            if w.w is not None:
                k, v = w.w
                if v > need.get(k, 0):
                    need[k] = v
            for k, v in w.readers.items():
                if v > need.get(k, 0):
                    need[k] = v
        out = []
        wmi = self.wm[issuer]
        for k in sorted(need):
            v = need[k]
            if k == issuer and issuer == "t":
                continue
            if v > wmi.get(k, 0):
                out.append((k, v))
                wmi[k] = v
        return out

    def op(self, eng, emit, reads=(), writes=(), inc=1):
        waits = self._waits(eng, reads, writes)
        if inc:
            self.cnt[eng] += inc
            val = self.cnt[eng]
            self.plan[eng].append((waits, emit, eng, inc))
        else:
            val = self.cnt[eng] + 1
            self.plan[eng].append((waits, emit, None, 0))
        for r in reads:
            if r.readers.get(eng, 0) < val:
                r.readers[eng] = val
        for w in writes:
            w.w = (eng, val)
            w.readers = {}

    def dma(self, q, emit, reads=(), writes=()):
        issuer = {"qg": "p", "qs": "y", "qa": "s"}[q]
        lane = f"{q}{self.qidx[q] % self.NLANE[q]}"
        self.qidx[q] += 1
        waits = self._waits(issuer, reads, writes)
        wmi = self.wm[issuer]
        if self.cnt[lane] > wmi.get(lane, 0):
            waits.append((lane, self.cnt[lane]))
            wmi[lane] = self.cnt[lane]
        self.cnt[lane] += 16
        val = self.cnt[lane]
        self.plan[issuer].append((waits, emit, lane, 16))
        for r in reads:
            if r.readers.get(lane, 0) < val:
                r.readers[lane] = val
        for w in writes:
            w.w = (lane, val)
            w.readers = {}

    def final_wait(self, eng, q):
        for i in range(self.NLANE[q]):
            lane = f"{q}{i}"
            if self.cnt[lane]:
                self.plan[eng].append(
                    ([(lane, self.cnt[lane])], lambda e: e.nop(), None, 0))

    def finalize(self, block, boot_v=0):
        sems = self.sem
        for ename, attr in (("v", "vector"), ("s", "scalar"), ("t", "tensor"),
                            ("p", "gpsimd"), ("y", "sync")):
            items = self.plan[ename]
            pre = [] if (ename == "v" or not boot_v) else [("v", boot_v)]

            def body(eng, items=items, pre=pre):
                for (k, v) in pre:
                    eng.wait_ge(sems[k], v)
                for waits, emit, semk, inc in items:
                    for (k, v) in waits:
                        eng.wait_ge(sems[k], v)
                    ins = emit(eng)
                    if semk is not None:
                        ins.then_inc(sems[semk], inc)
            getattr(block, attr)(body)


def build_program(NT=4, NTP=4, upto=None):
    TC = NT * TT
    TP = NTP * TT
    NTG = TC // 128
    nc = bass.Bass("TRN2", target_bir_lowering=False)

    dr = {}

    def din(name, shape, dt=F32):
        dr[name] = nc.dram_tensor(name, list(shape), dt, kind="ExternalInput")

    din("x", (D, TC))
    din("xp", (D, TP))
    din("w1", (D, HID))
    din("w2", (HID, D))
    din("W1s", (NES, D, HID))
    din("W2s", (NES, HID, D))
    din("W1r", (NER, D, HID))
    din("W2r", (NER, HID, D))
    din("A_r", (128, GN))
    din("B_r", (128, GN))
    din("C_r", (128, GN))
    din("s1_r", (128, G))
    din("s2_r", (128, G))
    din("D_r", (128, G))
    din("gs_r", (128, G))
    din("gr_r", (128, G))
    din("b2a_r", (128, G))
    din("b1a_r", (128, NH))
    din("b1s_r", (128, NES * NH))
    din("b1r_r", (128, NER * NH))
    din("b2r", (NER, D))
    din("b2s0", (1, D))
    din("b2s1", (1, D))
    din("cent_r", (128, G * NER))
    din("IDENT", (128, 128))
    din("ONESBF", (128, 128), BF16)
    din("DTREP", (128, GN))
    din("DTRI", (128, GN))
    din("RJREV", (128, TP))
    din("WMTC", (16, 128))
    out_h = nc.dram_tensor("out", [TC, D], F32, kind="ExternalOutput")
    uhat_d = nc.dram_tensor("uhatd", [128, G * TC], BF16, kind="Internal")
    dbg_h = {}
    if upto is not None:
        for nm, shape in (("dbg_x", [128, G * TC]),
                          ("dbg_m", [128, G * TC // 2]),
                          ("dbg_wtok", [128, NTG * NER]),
                          ("dbg_smid", [128, GN]),
                          ("dbg_otm", [128, NTG * 1024])):
            dbg_h[nm] = nc.dram_tensor(nm, shape, F32, kind="ExternalOutput")

    es = contextlib.ExitStack()
    with es:
        def sbt(name, shape, dt=F32):
            return es.enter_context(nc.sbuf_tensor(name, list(shape), dt))

        XR = sbt("XR", (128, G * TC))
        BIGC = max(4 * G * TT, 8 * TC)
        BIG = sbt("BIG", (128, BIGC))
        MXSC = max(G * TC // 2, TP + 5120 + NTP * 128 + 2 * GN,
                   G * TT + 4096)
        MXS = sbt("MXS", (128, MXSC))
        SSb = sbt("SSb", (128, 4 * TT + 2 * GN + TT))
        STAT = sbt("STAT", (128, 5 * TT + 64))
        WBUF = sbt("WBUF", (128, 4096))
        COEF = sbt("COEF", (128, 5 * GN + 8 * G))
        BIA = sbt("BIA", (128, (1 + NES + NER) * NH))
        B2AUG = sbt("B2AUG", (16, D))
        WTOK = sbt("WTOK", (128, NTG * NER))
        WMT = sbt("WMT", (16, 128))
        IDE = sbt("IDE", (128, 128))
        ONE = sbt("ONE", (128, 128), BF16)

        DTRO = TP + 5120 + NTP * 128
        DTR = MXS[:, DTRO:DTRO + GN]
        DTI = MXS[:, DTRO + GN:DTRO + 2 * GN]
        SCT = STAT[0:8, 3 * TT:4 * TT]

        ps = [es.enter_context(nc.psum_tensor(f"ps{i}", [128, TT], F32))
              for i in range(8)]

        b = Bld(nc, es)
        block = es.enter_context(nc.Block())

        CONSTV = sbt("CONSTV", (128, 16))
        constv_rg = Rg("constv")
        for i, val in enumerate((0.0, 1.0, -0.5, -1.0, EPS_LN, EPS_RMS,
                                 EPS_FFN)):
            b.op("v", (lambda e, ap=CONSTV[:, i:i + 1], v=val:
                       e.memset(ap, v)), writes=[constv_rg])
            nc.const_aps.register(F32, float(val), CONSTV[:, i:i + 1])
        boot_v = b.cnt["v"]

        xr_rg = [Rg(f"xr{i}") for i in range(G * TC // TT)]
        big_rg = [Rg(f"big{i}") for i in range(BIGC // TT)]
        mxs_rg = [Rg(f"mxs{i}") for i in range(MXSC // TT + 1)]

        def rgs(lst, c0, c1):
            return [lst[i] for i in range(c0 // TT, (c1 + TT - 1) // TT)]

        ss_s = [Rg(f"ss_s{i}") for i in range(4)]
        ss_tmp = Rg("ss_tmp")
        ss_scol = [Rg(f"scol{gn}") for gn in range(GN)]
        ss_smid = Rg("smid")
        st_rg = [Rg(f"st{i}") for i in range(5)]
        st_mic = Rg("stmic")
        wb_w1 = [Rg("wbw1a"), Rg("wbw1b")]
        wb_w2 = [Rg("wbw2a"), Rg("wbw2b")]
        coef_rg = Rg("coef")
        bia_rg = Rg("bia")
        b2aug_rg = Rg("b2aug")
        wtok_rg = [Rg(f"wtok{tg}") for tg in range(NTG)]
        wmt_rg = Rg("wmt")
        sct_rg = Rg("sct")
        const_rg = Rg("const")
        ps_rg = [Rg(f"psr{i}") for i in range(8)]
        uhd_rg = Rg("uhatd")
        out_rg = Rg("outh")
        dbg_rg = Rg("dbg")

        def TSc(out, in0, s1, s2, op0, op1=ALU.bypass, acc=None):
            return lambda e: e.tensor_scalar(out, in0, s1, s2, op0, op1,
                                             accum_out=acc)

        def TTo(out, a, c, op):
            return lambda e: e.tensor_tensor(out, a, c, op)

        def STT(out, in0, s, in1, op0, op1, acc=None):
            return lambda e: e.scalar_tensor_tensor(out, in0, s, in1, op0, op1,
                                                    accum_out=acc)

        def CPY(out, in0):
            return lambda e: e.tensor_copy(out, in0)

        def ACT(out, in0, fn, bias=0.0, scale=1.0, acc=None):
            return lambda e: e.activation(out, in0, fn, bias=bias, scale=scale,
                                          accum_out=acc)

        def DMA(out, in0):
            return lambda e: e.dma_start(out=out, in_=in0)

        def MSET(ap, v):
            return lambda e: e.memset(ap, v)

        def bcg(row_ap, n):
            return row_ap.unsqueeze(1).broadcast_to([128, G, n])

        def v3(ap, n):
            return ap.rearrange("p (dg t) -> p dg t", t=n)

        cLogdA = COEF[:, 0:GN]
        cdB = COEF[:, GN:2 * GN]
        cC = COEF[:, 2 * GN:3 * GN]
        cdA = COEF[:, 3 * GN:4 * GN]
        cCB = COEF[:, 4 * GN:5 * GN]
        V8 = 5 * GN

        def cv8(i):
            return COEF[:, V8 + i * G:V8 + (i + 1) * G]

        def srw(i, n=TT):
            return STAT[:, i * TT:i * TT + n]
        MIC = 5 * TT

        def mic(i, n=8):
            return STAT[:, MIC + i * 8:MIC + i * 8 + n]

        b.op("v", MSET(B2AUG[0:16, :], 0.0), writes=[b2aug_rg])
        for dst, src, rg in [
                (IDE[:, :], dr["IDENT"][:, :], const_rg),
                (WMT[0:16, :], dr["WMTC"][:, :], wmt_rg),
                (ONE[:, :], dr["ONESBF"][:, :], const_rg),
                (DTR, dr["DTREP"][:, :], const_rg),
                (DTI, dr["DTRI"][:, :], const_rg),
                (cLogdA, dr["A_r"][:, :], coef_rg),
                (cdB, dr["B_r"][:, :], coef_rg),
                (cC, dr["C_r"][:, :], coef_rg),
                (cv8(0), dr["s1_r"][:, :], coef_rg),
                (cv8(1), dr["s2_r"][:, :], coef_rg),
                (cv8(2), dr["D_r"][:, :], coef_rg),
                (cv8(3), dr["gs_r"][:, :], coef_rg),
                (cv8(4), dr["gr_r"][:, :], coef_rg),
                (cv8(5), dr["b2a_r"][:, :], coef_rg),
                (BIA[:, 0:NH], dr["b1a_r"][:, :], bia_rg),
                (BIA[:, NH:(1 + NES) * NH], dr["b1s_r"][:, :], bia_rg),
                (BIA[:, (1 + NES) * NH:(1 + NES + NER) * NH],
                 dr["b1r_r"][:, :], bia_rg),
                (B2AUG[0:8, :], dr["b2r"][:, :], b2aug_rg),
                (B2AUG[8:9, :], dr["b2s0"][:, :], b2aug_rg),
                (B2AUG[9:10, :], dr["b2s1"][:, :], b2aug_rg)]:
            b.dma("qs", DMA(dst, src), writes=[rg])

        xv = dr["x"][:, :].rearrange("(dg p) t -> p dg t", p=128)
        b.dma("qs", DMA(v3(XR[:, :], TC), xv), writes=list(xr_rg))

        b.op("s", ACT(cdA, cLogdA, AF.Exp), reads=[coef_rg], writes=[coef_rg])
        b.op("v", STT(cLogdA, cdA, -1.0, DTR, ALU.mult, ALU.mult),
             reads=[coef_rg, const_rg], writes=[coef_rg])
        b.op("s", ACT(cdA, cLogdA, AF.Exp), reads=[coef_rg], writes=[coef_rg])
        b.op("v", TSc(cCB, cdA, -1.0, 1.0, ALU.mult, ALU.add),
             reads=[coef_rg], writes=[coef_rg])
        b.op("v", TTo(cdB, cdB, cCB, ALU.mult), reads=[coef_rg],
             writes=[coef_rg])
        b.op("v", STT(cCB, cLogdA, -1.0, DTI, ALU.mult, ALU.mult),
             reads=[coef_rg, const_rg], writes=[coef_rg])
        b.op("v", lambda e: e.reciprocal(cCB, cCB), reads=[coef_rg],
             writes=[coef_rg])
        b.op("v", TTo(cdB, cdB, cCB, ALU.mult), reads=[coef_rg],
             writes=[coef_rg])
        b.op("v", TTo(cCB, cC, cdB, ALU.mult), reads=[coef_rg],
             writes=[coef_rg])
        for g in range(G):
            b.op("v", TSc(cCB[:, g * NS:(g + 1) * NS],
                          cCB[:, g * NS:(g + 1) * NS],
                          cv8(0)[:, g:g + 1], None, ALU.mult),
                 reads=[coef_rg], writes=[coef_rg])
        b.op("v", TTo(cv8(6), cv8(2), cv8(0), ALU.mult),
             reads=[coef_rg], writes=[coef_rg])


        def stats(xbf_ap, xbf_rgs, eps2=None):
            def e_s1(e):
                last = None
                for dg in range(G):
                    last = e.matmul(ps[6][:, 0:TT], ONE[:, :],
                                    xbf_ap[:, dg * TT:(dg + 1) * TT],
                                    start=(dg == 0), stop=(dg == G - 1))
                return last
            b.op("t", e_s1, reads=xbf_rgs + [const_rg], writes=[ps_rg[6]])
            b.op("s", ACT(xbf_ap, xbf_ap, AF.Square), reads=xbf_rgs,
                 writes=xbf_rgs)

            def e_s2(e):
                last = None
                for dg in range(G):
                    last = e.matmul(ps[7][:, 0:TT], ONE[:, :],
                                    xbf_ap[:, dg * TT:(dg + 1) * TT],
                                    start=(dg == 0), stop=(dg == G - 1))
                return last
            b.op("t", e_s2, reads=xbf_rgs + [const_rg], writes=[ps_rg[7]])
            var, rstd, mu, ntm = srw(0), srw(1), srw(2), srw(3)
            rv, rr, rm, rn = st_rg[0], st_rg[1], st_rg[2], st_rg[3]
            b.op("v", TSc(mu, ps[6][:, 0:TT], 1.0 / D, None, ALU.mult),
                 reads=[ps_rg[6]], writes=[rm])
            b.op("v", TTo(var, mu, mu, ALU.mult), reads=[rm], writes=[rv])
            b.op("v", STT(var, ps[7][:, 0:TT], 1.0 / D, var, ALU.mult,
                          ALU.subtract), reads=[ps_rg[7], rv], writes=[rv])
            if eps2 is not None:
                inv2 = srw(4)
                ri = st_rg[4]
                b.op("v", TSc(inv2, var, EPS_LN, None, ALU.add),
                     reads=[rv], writes=[ri])
                b.op("v", lambda e, o=inv2, i=inv2: e.reciprocal(o, i),
                     reads=[ri], writes=[ri])
                b.op("v", TTo(inv2, inv2, var, ALU.mult),
                     reads=[ri, rv], writes=[ri])
                b.op("s", ACT(inv2, inv2, AF.Ln, bias=eps2),
                     reads=[ri], writes=[ri])
                b.op("s", ACT(inv2, inv2, AF.Exp, scale=-0.5),
                     reads=[ri], writes=[ri])
            b.op("v", TSc(var, var, EPS_LN, None, ALU.add),
                 reads=[rv], writes=[rv])
            b.op("s", ACT(rstd, var, AF.Ln), reads=[rv], writes=[rr])
            b.op("s", ACT(rstd, rstd, AF.Exp, scale=-0.5),
                 reads=[rr], writes=[rr])
            b.op("v", TTo(ntm, rstd, rstd, ALU.mult), reads=[rr], writes=[rn])
            b.op("v", TTo(ntm, ntm, var, ALU.mult), reads=[rn, rv],
                 writes=[rn])
            b.op("v", TSc(ntm, ntm, -0.5, 1.5, ALU.mult, ALU.add),
                 reads=[rn], writes=[rn])
            b.op("v", TTo(rstd, rstd, ntm, ALU.mult),
                 reads=[rr, rn], writes=[rr])
            return var, rstd, mu

        def stats_sq_only(xbf_ap, xbf_rgs, eps):
            def e_s2(e):
                last = None
                for dg in range(G):
                    last = e.matmul(ps[7][:, 0:TT], ONE[:, :],
                                    xbf_ap[:, dg * TT:(dg + 1) * TT],
                                    start=(dg == 0), stop=(dg == G - 1))
                return last
            b.op("t", e_s2, reads=xbf_rgs + [const_rg], writes=[ps_rg[7]])
            inv = srw(4)
            ri = st_rg[4]
            b.op("v", TSc(inv, ps[7][:, 0:TT], 1.0 / D, None, ALU.mult),
                 reads=[ps_rg[7]], writes=[ri])
            b.op("s", ACT(inv, inv, AF.Ln, bias=eps), reads=[ri], writes=[ri])
            b.op("s", ACT(inv, inv, AF.Exp, scale=-0.5),
                 reads=[ri], writes=[ri])
            return inv, ri

        w1slot = [(STAT[:, 0:1024].bitcast(BF16), [st_rg[0], st_rg[1]]),
                  (STAT[:, 1024:2048].bitcast(BF16), [st_rg[2], st_rg[3]])]
        w2slot = [(WBUF[:, 0:2048].bitcast(BF16), [wb_w2[0], wb_w2[1]]),
                  (WBUF[:, 2048:4096].bitcast(BF16), [wb_w1[0], wb_w1[1]])]


        smid_ap = SSb[:, 4 * TT + GN:4 * TT + 2 * GN]
        b.op("v", MSET(smid_ap, 0.0), writes=[ss_smid])
        rjr = MXS[:, 0:TP]
        rj_rg = rgs(mxs_rg, 0, TP)
        b.dma("qs", DMA(rjr, dr["RJREV"][:, :]), writes=rj_rg)
        xpv = dr["xp"][:, :].rearrange("(dg p) t -> p dg t", p=128)

        xbfo = TP
        expwo = TP + 2048
        NEXPW = 4
        dumpo = expwo + NEXPW * TT
        smpo = dumpo + TT
        xbf_rg_p1 = rgs(mxs_rg, xbfo, xbfo + 2048)
        expw_rg = [rgs(mxs_rg, expwo + i * TT, expwo + (i + 1) * TT)[0]
                   for i in range(NEXPW)]
        dump_rg = rgs(mxs_rg, dumpo, dumpo + TT)
        smp_rg = [rgs(mxs_rg, smpo + i * 128, smpo + (i + 1) * 128)[0]
                  for i in range(NTP)]

        for pt in range(NTP):
            sl = pt % 2
            xpo = G * TT * (2 + sl)
            xp_ap = BIG[:, xpo:xpo + G * TT]
            xp_rgs = rgs(big_rg, xpo, xpo + G * TT)
            b.dma("qs", DMA(v3(xp_ap, TT), xpv[:, :, pt * TT:(pt + 1) * TT]),
                  writes=xp_rgs)
            xbf = MXS[:, xbfo:xbfo + 2048].bitcast(BF16)
            b.dma("qg", DMA(xbf.rearrange("p (dg t) -> p dg t", t=TT),
                            xpv[:, :, pt * TT:(pt + 1) * TT]),
                  writes=xbf_rg_p1)
            var, rstd, mu = stats(xbf, xbf_rg_p1, eps2=EPS_RMS)
            inv2 = srw(4)
            comb, rc = srw(3), st_rg[3]
            mucomb, rmc = srw(0), st_rg[0]
            b.op("v", TTo(comb, rstd, inv2, ALU.mult),
                 reads=[st_rg[1], st_rg[4]], writes=[rc])
            b.op("v", TTo(mucomb, mu, comb, ALU.mult),
                 reads=[st_rg[2], rc], writes=[rmc])
            upo = G * TT * sl
            up_ap = BIG[:, upo:upo + G * TT]
            up_rgs = rgs(big_rg, upo, upo + G * TT)
            b.op("v", TTo(v3(up_ap, TT), v3(xp_ap, TT), bcg(comb, TT),
                          ALU.mult), reads=xp_rgs + [rc], writes=up_rgs)
            b.op("v", TTo(v3(up_ap, TT), v3(up_ap, TT), bcg(mucomb, TT),
                          ALU.subtract), reads=up_rgs + [rmc], writes=up_rgs)
            smp_ap = MXS[:, smpo + pt * 128:smpo + (pt + 1) * 128]
            for gn in range(GN):
                g = gn // NS
                esl = gn % NEXPW
                ew = MXS[:, expwo + esl * TT:expwo + (esl + 1) * TT]
                b.op("s", ACT(ew, rjr[:, pt * TT:(pt + 1) * TT], AF.Exp,
                              scale=cLogdA[:, gn:gn + 1]),
                     reads=rj_rg + [coef_rg], writes=[expw_rg[esl]])
                b.op("v", STT(MXS[:, dumpo:dumpo + TT], ew, 1.0,
                              up_ap[:, g * TT:(g + 1) * TT], ALU.mult,
                              ALU.mult, acc=smp_ap[:, gn:gn + 1]),
                     reads=[expw_rg[esl], up_rgs[g]],
                     writes=dump_rg + [smp_rg[pt]])
        for pt in range(NTP):
            smp_ap = MXS[:, smpo + pt * 128:smpo + (pt + 1) * 128]
            b.op("v", TTo(smid_ap, smid_ap, smp_ap, ALU.add),
                 reads=[smp_rg[pt], ss_smid], writes=[ss_smid])

        USO, YO, XBO = 0, G * TT, 2 * G * TT
        UBO = 3 * G * TT
        us_ap = BIG[:, USO:USO + G * TT]
        y_ap = BIG[:, YO:YO + G * TT]
        us_rgs = rgs(big_rg, USO, USO + G * TT)
        y_rgs = rgs(big_rg, YO, YO + G * TT)
        m_bf = MXS[:, 0:G * TC // 2].bitcast(BF16)

        def m_rgs(dg, t):
            return rgs(mxs_rg, dg * TC // 2 + t * TT // 2,
                       dg * TC // 2 + (t + 1) * TT // 2)

        for t in range(NT):
            sl = t % 2
            xbf = BIG[:, XBO + sl * 2048:XBO + (sl + 1) * 2048].bitcast(BF16)
            xbf_rgs = rgs(big_rg, XBO + sl * 2048, XBO + (sl + 1) * 2048)
            b.dma("qg", DMA(xbf.rearrange("p (dg t) -> p dg t", t=TT),
                            xv[:, :, t * TT:(t + 1) * TT]), writes=xbf_rgs)
            var, rstd, mu = stats(xbf, xbf_rgs, eps2=EPS_RMS)
            inv2, ri2 = srw(4), st_rg[4]
            comb, rc = srw(3), st_rg[3]
            b.op("v", TTo(comb, rstd, inv2, ALU.mult),
                 reads=[st_rg[1], ri2], writes=[rc])
            mucomb, rmc = srw(1), st_rg[1]
            b.op("v", TTo(mucomb, mu, comb, ALU.mult),
                 reads=[st_rg[2], rc], writes=[rmc])
            xt = v3(XR[:, :], TC)[:, :, t * TT:(t + 1) * TT]
            xt_rgs = [xr_rg[(dg * TC + t * TT) // TT] for dg in range(G)]
            b.op("v", TTo(v3(us_ap, TT), xt, bcg(comb, TT), ALU.mult),
                 reads=xt_rgs + [rc], writes=us_rgs)
            b.op("v", TTo(v3(us_ap, TT), v3(us_ap, TT), bcg(mucomb, TT),
                          ALU.subtract), reads=us_rgs + [rmc], writes=us_rgs)
            rc2, rrc2 = srw(2), st_rg[2]
            b.op("v", lambda e, o=rc2, i=inv2: e.reciprocal(o, i),
                 reads=[ri2], writes=[rrc2])
            b.op("v", MSET(y_ap, 0.0), writes=y_rgs)
            yp_ap = BIG[:, UBO:UBO + G * TT]
            yp_rgs = rgs(big_rg, UBO, UBO + G * TT)
            b.op("p", lambda e: e.memset(yp_ap, 0.0), writes=yp_rgs)
            tmp_p = SSb[:, 4 * TT + 2 * GN:5 * TT + 2 * GN]
            for gn in range(GN):
                g = gn // NS
                ssl = gn % 4
                s_ap = SSb[:, ssl * TT:(ssl + 1) * TT]
                if t == 0:
                    init = smid_ap[:, gn:gn + 1]
                    init_rg = ss_smid
                else:
                    init = SSb[:, 4 * TT + gn:4 * TT + gn + 1]
                    init_rg = ss_scol[gn]
                b.op("v", lambda e, o=s_ap, gn=gn, g=g, ini=init:
                     e.tensor_tensor_scan(
                         o, cdA[:, gn:gn + 1].broadcast_to([128, TT]),
                         us_ap[:, g * TT:(g + 1) * TT], ini,
                         ALU.mult, ALU.add),
                     reads=[coef_rg, init_rg, us_rgs[g]],
                     writes=[ss_s[ssl]])
                if gn % 2 == 0:
                    b.op("v", STT(y_ap[:, g * TT:(g + 1) * TT], s_ap,
                                  cCB[:, gn:gn + 1],
                                  y_ap[:, g * TT:(g + 1) * TT],
                                  ALU.mult, ALU.add),
                         reads=[ss_s[ssl], coef_rg, y_rgs[g]],
                         writes=[y_rgs[g]])
                else:
                    b.op("p", TTo(tmp_p, s_ap,
                                  cCB[:, gn:gn + 1].broadcast_to([128, TT]),
                                  ALU.mult),
                         reads=[ss_s[ssl], coef_rg], writes=[ss_tmp])
                    b.op("p", TTo(yp_ap[:, g * TT:(g + 1) * TT],
                                  yp_ap[:, g * TT:(g + 1) * TT], tmp_p,
                                  ALU.add),
                         reads=[ss_tmp, yp_rgs[g]], writes=[yp_rgs[g]])
                if t < NT - 1:
                    b.op("p", CPY(SSb[:, 4 * TT + gn:4 * TT + gn + 1],
                                  s_ap[:, TT - 1:TT]),
                         reads=[ss_s[ssl]], writes=[ss_scol[gn]])
            b.op("v", TTo(v3(y_ap, TT), v3(y_ap, TT), v3(yp_ap, TT), ALU.add),
                 reads=y_rgs + yp_rgs, writes=y_rgs)
            for dg in range(G):
                b.op("v", STT(y_ap[:, dg * TT:(dg + 1) * TT],
                              us_ap[:, dg * TT:(dg + 1) * TT],
                              cv8(6)[:, dg:dg + 1],
                              y_ap[:, dg * TT:(dg + 1) * TT],
                              ALU.mult, ALU.add),
                     reads=[us_rgs[dg], coef_rg, y_rgs[dg]],
                     writes=[y_rgs[dg]])
            b.op("v", TTo(v3(us_ap, TT), v3(us_ap, TT), bcg(rc2, TT),
                          ALU.mult), reads=us_rgs + [rrc2], writes=us_rgs)
            b.op("v", TTo(v3(y_ap, TT), v3(y_ap, TT), v3(us_ap, TT), ALU.add),
                 reads=y_rgs + us_rgs, writes=y_rgs)
            xbf2 = BIG[:, XBO + sl * 2048:XBO + (sl + 1) * 2048].bitcast(BF16)
            b.op("s", ACT(xbf2, y_ap, AF.Square), reads=y_rgs,
                 writes=xbf_rgs)
            inv3, ri3 = stats_sq_only(xbf2, xbf_rgs, EPS_RMS)
            b.op("v", TTo(v3(us_ap, TT), v3(y_ap, TT), bcg(inv3, TT),
                          ALU.mult), reads=y_rgs + [ri3], writes=us_rgs)
            for dg in range(G):
                b.op("s", ACT(m_bf[:, dg * TC + t * TT:dg * TC + (t + 1) * TT],
                              us_ap[:, dg * TT:(dg + 1) * TT], AF.Copy,
                              scale=cv8(1)[:, dg:dg + 1]),
                     reads=[us_rgs[dg], coef_rg], writes=m_rgs(dg, t))
            b.op("v", TTo(xt, xt, v3(y_ap, TT), ALU.add),
                 reads=xt_rgs + y_rgs, writes=xt_rgs)

        def dump(which):
            if "x" in which:
                b.dma("qs", DMA(dbg_h["dbg_x"][:, :], XR[:, :]),
                      reads=list(xr_rg), writes=[dbg_rg])
            if "m" in which:
                b.dma("qs", DMA(dbg_h["dbg_m"][:, :],
                                MXS[:, 0:G * TC // 2]),
                      reads=list(mxs_rg[:G * TC // 2 // TT]), writes=[dbg_rg])
            if "smid" in which:
                b.dma("qs", DMA(dbg_h["dbg_smid"][:, :], smid_ap),
                      reads=[ss_smid], writes=[dbg_rg])
            if "wtok" in which:
                b.dma("qs", DMA(dbg_h["dbg_wtok"][:, :], WTOK[:, :]),
                      reads=list(wtok_rg), writes=[dbg_rg])
            if "otm" in which:
                b.dma("qs", DMA(dbg_h["dbg_otm"][:, :], BIG[:, 0:NTG * 1024]),
                      reads=rgs(big_rg, 0, NTG * 1024), writes=[dbg_rg])
            b.final_wait("y", "qs")

        if upto == "P2":
            dump(("x", "m", "smid"))
            b.finalize(block, boot_v)
            return nc

        NQ = 4
        QH = HID // NQ
        bigbf = BIG[:, :].bitcast(BF16)
        m_all = rgs(mxs_rg, 0, G * TC // 2)

        def h3_ap(slab, hsub, tt):
            c = slab * (G * TC) + hsub * TC + tt * TT
            return bigbf[:, c:c + TT], rgs(big_rg, c // 2, (c + TT) // 2)

        for q in range(NQ):
            slab = q % 2
            for hc in range(QH // 256):
                wsl = (q * (QH // 256) + hc) % 2
                w1c, w1rgs = w1slot[wsl]
                h0 = q * QH + hc * 256
                src = dr["w1"][:, h0:h0 + 256].rearrange(
                    "(dg p) h -> p dg h", p=128)
                b.dma("qg", DMA(w1c.rearrange("p (dg h) -> p dg h", h=256),
                                src), writes=w1rgs)
                for hs in range(2):
                    hsub = hc * 2 + hs
                    for tt in range(NT):
                        pp = ps[(hsub * NT + tt) % 3]
                        pr = ps_rg[(hsub * NT + tt) % 3]

                        def e_mm1(e, pp=pp, w1c=w1c, hs=hs, tt=tt):
                            last = None
                            for dg in range(G):
                                last = e.matmul(
                                    pp[:, 0:TT],
                                    w1c[:, dg * 256 + hs * 128:
                                        dg * 256 + (hs + 1) * 128],
                                    m_bf[:, dg * TC + tt * TT:
                                         dg * TC + (tt + 1) * TT],
                                    start=(dg == 0), stop=(dg == G - 1))
                            return last
                        b.op("t", e_mm1, reads=w1rgs + m_all,
                             writes=[pr])
                        hap, h_rgs = h3_ap(slab, hsub, tt)
                        b.op("s", ACT(hap, pp[:, 0:TT], AF.Gelu,
                                      bias=BIA[:, q * 8 + hsub:
                                               q * 8 + hsub + 1]),
                             reads=[pr, bia_rg], writes=h_rgs)
            for tt in range(NT):
                for hsub in range(8):
                    psl = (hsub // 4) % 2
                    if hsub % 4 == 0:
                        w2p, w2rgs = w2slot[psl]
                        hrow = q * QH + hsub * 128
                        b.dma("qg", DMA(
                            w2p.rearrange("p (h2 d) -> p h2 d", d=D),
                            dr["w2"][hrow:hrow + 512, :].rearrange(
                                "(h2 p) d -> p h2 d", p=128)),
                            writes=w2rgs)
                    w2p, w2rgs = w2slot[psl]
                    w2t = w2p[:, (hsub % 4) * D:(hsub % 4 + 1) * D]
                    hap, h_rgs = h3_ap(slab, hsub, tt)

                    def e_mm2(e, w2t=w2t, hap=hap, hsub=hsub):
                        last = None
                        for dmg in range(G):
                            last = e.matmul(ps[dmg][:, 0:TT],
                                            w2t[:, dmg * 128:(dmg + 1) * 128],
                                            hap, start=(hsub == 0),
                                            stop=(hsub == 7))
                        return last
                    b.op("t", e_mm2, reads=w2rgs + h_rgs,
                         writes=[ps_rg[dmg] for dmg in range(G)])
                for dmg in range(G):
                    xap = XR[:, dmg * TC + tt * TT:dmg * TC + (tt + 1) * TT]
                    xrg = xr_rg[(dmg * TC + tt * TT) // TT]
                    if q < NQ - 1:
                        b.op("v", TTo(xap, xap, ps[dmg][:, 0:TT], ALU.add),
                             reads=[ps_rg[dmg], xrg], writes=[xrg])
                    else:
                        b.op("v", STT(xap, ps[dmg][:, 0:TT],
                                      cv8(5)[:, dmg:dmg + 1], xap,
                                      ALU.add, ALU.add),
                             reads=[ps_rg[dmg], coef_rg, xrg], writes=[xrg])

        if upto == "P3":
            dump(("x", "m"))
            b.finalize(block, boot_v)
            return nc

        uo = 0
        u_ap = MXS[:, uo:uo + G * TT]
        u_rgs = rgs(mxs_rg, uo, uo + G * TT)
        uh_o = G * TT
        uhs_rg = [rgs(mxs_rg, uh_o, uh_o + 2048),
                  rgs(mxs_rg, uh_o + 2048, uh_o + 4096)]
        cent_sb = WBUF[:, 0:64]
        b.dma("qs", DMA(cent_sb, dr["cent_r"][:, :]), writes=[wb_w2[0]])
        xbf4 = WBUF[:, 64:64 + 2048].bitcast(BF16)
        xbf4_rgs = [wb_w2[0], wb_w2[1], wb_w1[0], wb_w1[1]]

        for t in range(NT):
            xt = v3(XR[:, :], TC)[:, :, t * TT:(t + 1) * TT]
            xt_rgs = [xr_rg[(dg * TC + t * TT) // TT] for dg in range(G)]
            b.op("v", CPY(xbf4.rearrange("p (dg t) -> p dg t", t=TT), xt),
                 reads=xt_rgs, writes=xbf4_rgs)
            var, rstd, mu = stats(xbf4, xbf4_rgs, eps2=EPS_FFN)
            inv4, ri4 = srw(4), st_rg[4]
            mucomb, rmc = srw(3), st_rg[3]
            b.op("v", TTo(mucomb, mu, rstd, ALU.mult),
                 reads=[st_rg[2], st_rg[1]], writes=[rmc])
            b.op("v", TTo(v3(u_ap, TT), xt, bcg(rstd, TT), ALU.mult),
                 reads=xt_rgs + [st_rg[1]], writes=u_rgs)
            b.op("v", TTo(v3(u_ap, TT), v3(u_ap, TT), bcg(mucomb, TT),
                          ALU.subtract), reads=u_rgs + [rmc], writes=u_rgs)

            def e_sc(e):
                last = None
                for dg in range(G):
                    last = e.matmul(ps[3][0:8, 0:TT],
                                    cent_sb[:, dg * 8:(dg + 1) * 8],
                                    u_ap[:, dg * TT:(dg + 1) * TT],
                                    start=(dg == 0), stop=(dg == G - 1))
                return last
            b.op("t", e_sc, reads=u_rgs + [wb_w2[0]], writes=[ps_rg[3]])
            b.op("v", CPY(SCT[0:8, 0:TT], ps[3][0:8, 0:TT]),
                 reads=[ps_rg[3]], writes=[st_rg[3]])
            for tgl in range(TT // 128):
                tg = t * (TT // 128) + tgl
                b.op("t", lambda e, tgl=tgl: e.transpose(
                    ps[3][:, 0:8], SCT[0:8, tgl * 128:(tgl + 1) * 128],
                    IDE[0:8, 0:8]), reads=[st_rg[3], const_rg],
                    writes=[ps_rg[3]])
                lg = mic(0)
                b.op("v", CPY(lg, ps[3][:, 0:8]), reads=[ps_rg[3]],
                     writes=[st_mic])
                m1 = mic(4, 1)
                b.op("v", lambda e, o=m1, i=lg: e.reduce_max(o, i, AX.X),
                     reads=[st_mic], writes=[st_mic])
                mk1 = mic(1)
                b.op("v", TSc(mk1, lg, m1, None, ALU.is_ge),
                     reads=[st_mic], writes=[st_mic])
                t2 = mic(2)
                b.op("v", STT(t2, mk1, -1e30, lg, ALU.mult, ALU.add),
                     reads=[st_mic], writes=[st_mic])
                m2 = mic(5, 1)
                b.op("v", lambda e, o=m2, i=t2: e.reduce_max(o, i, AX.X),
                     reads=[st_mic], writes=[st_mic])
                mk = mic(1)
                b.op("v", TSc(mk, lg, m2, None, ALU.is_ge),
                     reads=[st_mic], writes=[st_mic])
                sg = mic(3)
                b.op("s", ACT(sg, lg, AF.Exp, scale=-1.0),
                     reads=[st_mic], writes=[st_mic])
                b.op("v", TSc(sg, sg, 1.0, None, ALU.add),
                     reads=[st_mic], writes=[st_mic])
                b.op("v", lambda e, o=sg, i=sg: e.reciprocal(o, i),
                     reads=[st_mic], writes=[st_mic])
                dn = mic(6, 1)
                b.op("v", lambda e, o=dn, i=sg: e.reduce_sum(o, i, AX.X),
                     reads=[st_mic], writes=[st_mic])
                wv = mic(2)
                b.op("v", TTo(wv, sg, mk, ALU.mult), reads=[st_mic],
                     writes=[st_mic])
                b.op("v", lambda e, o=dn, i=dn: e.reciprocal(o, i),
                     reads=[st_mic], writes=[st_mic])
                b.op("v", TSc(WTOK[:, tg * 8:(tg + 1) * 8], wv, dn, None,
                              ALU.mult), reads=[st_mic],
                     writes=[wtok_rg[tg]])
            usl = t % 2
            uh_ap = MXS[:, uh_o + usl * 2048:uh_o + (usl + 1) * 2048]
            uh_bf = uh_ap.bitcast(BF16)
            b.op("v", TTo(uh_bf.rearrange("p (dg t) -> p dg t", t=TT),
                          v3(u_ap, TT), bcg(inv4, TT), ALU.mult),
                 reads=u_rgs + [ri4], writes=uhs_rg[usl])
            b.dma("qg", DMA(
                dr_uhat_view(uhat_d, t, TC), uh_bf.rearrange(
                    "p (dg t) -> p dg t", t=TT)),
                reads=uhs_rg[usl], writes=[uhd_rg])
            b.op("v", TTo(v3(u_ap, TT), v3(u_ap, TT), xt, ALU.add),
                 reads=u_rgs + xt_rgs, writes=u_rgs)
            for tgl in range(TT // 128):
                tg = t * (TT // 128) + tgl
                for dg in range(G):
                    pt_ = ps[4 + dg // 4]
                    b.op("t", lambda e, dg=dg, tgl=tgl, pt_=pt_: e.transpose(
                        pt_[:, (dg % 4) * 128:(dg % 4 + 1) * 128],
                        u_ap[:, dg * TT + tgl * 128:dg * TT + (tgl + 1) * 128],
                        IDE[:, :]),
                        reads=u_rgs + [const_rg],
                        writes=[ps_rg[4 + dg // 4]],
                        inc=(1 if dg % 4 == 3 else 0))
                for hh in range(2):
                    b.op("v", CPY(BIG[:, tg * 1024 + hh * 512:
                                       tg * 1024 + (hh + 1) * 512],
                                  ps[4 + hh][:, 0:TT]),
                         reads=[ps_rg[4 + hh]],
                         writes=rgs(big_rg, tg * 1024 + hh * 512,
                                    tg * 1024 + (hh + 1) * 512))

        if upto == "P4":
            dump(("wtok", "otm"))
            b.finalize(block, boot_v)
            return nc

        xn_bf = MXS[:, 0:G * TC // 2].bitcast(BF16)
        xn_all = rgs(mxs_rg, 0, G * TC // 2)
        xrbf = XR[:, :].bitcast(BF16)

        n_slab5 = max(1, (G * TC) // 8192)

        def h5(slab, hsub):
            c = slab * 16384 + hsub * TT
            return (xrbf[:, c:c + TT],
                    rgs(xr_rg, c // 2, (c + TT) // 2))

        def load_xn(gvec_idx):
            for half in range(2):
                b.dma("qg", DMA(
                    xn_bf[:, half * (G * TC // 2):(half + 1) * (G * TC // 2)],
                    uhat_d[:, half * (G * TC // 2):(half + 1) * (G * TC // 2)]),
                    reads=[uhd_rg], writes=xn_all)
            for dg in range(G):
                b.op("v", TSc(xn_bf[:, dg * TC:(dg + 1) * TC],
                              xn_bf[:, dg * TC:(dg + 1) * TC],
                              cv8(gvec_idx)[:, dg:dg + 1], None, ALU.mult),
                     reads=xn_all + [coef_rg], writes=xn_all)

        def expert(eidx, w1_src, w2_src2, b1col, wcol):
            PSZ = min(2, n_slab5)
            for pq in range(NT // PSZ):
                tqs = [pq * PSZ + i for i in range(PSZ)]
                for quad in range(2):
                    w2p, w2rgs = w2slot[quad % 2]
                    b.dma("qg", DMA(
                        w2p.rearrange("p (h2 d) -> p h2 d", d=D),
                        w2_src2(quad * 512)), writes=w2rgs)
                for hc in range(HID // 256):
                    wsl = hc % 2
                    w1c, w1rgs = w1slot[wsl]
                    b.dma("qg", DMA(
                        w1c.rearrange("p (dg h) -> p dg h", h=256),
                        w1_src(hc * 256, 256)), writes=w1rgs)
                    for hs in range(2):
                        hsub = hc * 2 + hs
                        for tq in tqs:
                            pidx = hsub * PSZ + (tq % PSZ)
                            pp = ps[pidx % 3]
                            pr = ps_rg[pidx % 3]

                            def e_mm1(e, pp=pp, w1c=w1c, hs=hs, tq=tq):
                                last = None
                                for dg in range(G):
                                    last = e.matmul(
                                        pp[:, 0:TT],
                                        w1c[:, dg * 256 + hs * 128:
                                            dg * 256 + (hs + 1) * 128],
                                        xn_bf[:, dg * TC + tq * TT:
                                              dg * TC + (tq + 1) * TT],
                                        start=(dg == 0), stop=(dg == G - 1))
                                return last
                            b.op("t", e_mm1, reads=w1rgs + xn_all,
                                 writes=[pr])
                            hap, h_rgs = h5(tq % n_slab5, hsub)
                            b.op("s", ACT(hap, pp[:, 0:TT], AF.Gelu,
                                          bias=BIA[:, b1col + hsub:
                                                   b1col + hsub + 1]),
                                 reads=[pr, bia_rg], writes=h_rgs)
                for ti, tq in enumerate(tqs):
                    mm2t_body(tq, tq % n_slab5, w2_src2, wcol,
                              npref=(2 if ti == 0 else 0))

        def mm2t_body(tq, slab, w2_src2, wcol, npref=0):
            for hsub in range(NH):
                psl = (hsub // 4) % 2
                if hsub % 4 == 0:
                    w2p, w2rgs = w2slot[psl]
                    if hsub // 4 >= npref:
                        b.dma("qg", DMA(
                            w2p.rearrange("p (h2 d) -> p h2 d", d=D),
                            w2_src2(hsub * 128)), writes=w2rgs)
                w2p, w2rgs = w2slot[psl]
                w2t = w2p[:, (hsub % 4) * D:(hsub % 4 + 1) * D]
                hap, h_rgs = h5(slab, hsub)

                def e_mm2(e, w2t=w2t, hap=hap, hsub=hsub):
                    last = None
                    for tgl in range(4):
                        for dmh in range(2):
                            last = e.matmul(
                                ps[tgl * 2 + dmh][:, 0:TT],
                                hap[:, tgl * 128:(tgl + 1) * 128],
                                w2t[:, dmh * 512:(dmh + 1) * 512],
                                start=(hsub == 0), stop=(hsub == NH - 1))
                    return last
                b.op("t", e_mm2, reads=w2rgs + h_rgs,
                     writes=[ps_rg[i] for i in range(8)])
            for tgl in range(4):
                tg = tq * 4 + tgl
                for dmh in range(2):
                    oap = BIG[:, tg * 1024 + dmh * 512:
                              tg * 1024 + (dmh + 1) * 512]
                    orgs = rgs(big_rg, tg * 1024 + dmh * 512,
                               tg * 1024 + (dmh + 1) * 512)
                    scl = (1.0 if wcol is None else
                           WTOK[:, tg * 8 + wcol:tg * 8 + wcol + 1])
                    rds = [ps_rg[tgl * 2 + dmh]] + orgs
                    if wcol is not None:
                        rds.append(wtok_rg[tg])
                    b.op("v", STT(oap, ps[tgl * 2 + dmh][:, 0:TT], scl,
                                  oap, ALU.mult, ALU.add),
                         reads=rds, writes=orgs)

        load_xn(3)
        for e_i in range(NES):
            expert(e_i,
                   lambda h0, n, e_i=e_i: dr["W1s"][e_i, :, h0:h0 + n]
                   .rearrange("(dg p) h -> p dg h", p=128),
                   lambda hr, e_i=e_i: dr["W2s"][e_i, hr:hr + 512, :]
                   .rearrange("(h2 p) d -> p h2 d", p=128),
                   NH + e_i * NH, None)
        load_xn(4)
        for e_i in range(NER):
            expert(e_i,
                   lambda h0, n, e_i=e_i: dr["W1r"][e_i, :, h0:h0 + n]
                   .rearrange("(dg p) h -> p dg h", p=128),
                   lambda hr, e_i=e_i: dr["W2r"][e_i, hr:hr + 512, :]
                   .rearrange("(h2 p) d -> p h2 d", p=128),
                   (1 + NES) * NH + e_i * NH, e_i)

        for tg in range(NTG):
            b.op("t", lambda e, tg=tg: e.transpose(
                ps[3][:, 0:8]
                if False else ps[3][0:8, 0:128],
                WTOK[:, tg * 8:(tg + 1) * 8], IDE[:, :]),
                reads=[wtok_rg[tg], const_rg], writes=[ps_rg[3]])
            b.op("v", CPY(WMT[0:8, :], ps[3][0:8, 0:128]),
                 reads=[ps_rg[3]], writes=[wmt_rg])
            for dmh in range(2):
                b.op("t", lambda e, dmh=dmh: e.matmul(
                    ps[4 + dmh][:, 0:TT], WMT[0:16, :],
                    B2AUG[0:16, dmh * 512:(dmh + 1) * 512],
                    start=True, stop=True),
                    reads=[wmt_rg, b2aug_rg], writes=[ps_rg[4 + dmh]])
            for dmh in range(2):
                oap = BIG[:, tg * 1024 + dmh * 512:
                          tg * 1024 + (dmh + 1) * 512]
                orgs = rgs(big_rg, tg * 1024 + dmh * 512,
                           tg * 1024 + (dmh + 1) * 512)
                b.op("v", TTo(oap, oap, ps[4 + dmh][:, 0:TT], ALU.add),
                     reads=[ps_rg[4 + dmh]] + orgs, writes=orgs)
            b.dma("qa", DMA(out_h[tg * 128:(tg + 1) * 128, :],
                            BIG[:, tg * 1024:(tg + 1) * 1024]),
                  reads=rgs(big_rg, tg * 1024, (tg + 1) * 1024),
                  writes=[out_rg])
        b.final_wait("y", "qa")
        b.finalize(block, boot_v)
        return nc


def dr_uhat_view(uhat_d, t, TC):
    return uhat_d[:, :].rearrange("p (dg t) -> p dg t", t=TC)[
        :, :, t * TT:(t + 1) * TT]


def _prep_shared(inp):
    f32 = np.float32
    r = {}
    rs = lambda a: np.ascontiguousarray(a)
    A = np.asarray(inp["L0_attn_A_ssm"], f32)
    B = np.asarray(inp["L0_attn_B_ssm"], f32)
    C = np.asarray(inp["L0_attn_C_ssm"], f32)
    resh = lambda M: rs(M.reshape(G, 128, NS).transpose(1, 0, 2)
                        .reshape(128, GN))
    r["A_r"], r["B_r"], r["C_r"] = resh(A), resh(B), resh(C)
    vec8 = lambda v: rs(np.asarray(v, f32).reshape(G, 128).T)
    r["s1_r"] = vec8(inp["L0_attn_scale1"])
    r["s2_r"] = vec8(inp["L0_attn_scale2"])
    r["D_r"] = vec8(inp["L0_attn_D_ssm"])
    r["gs_r"] = vec8(inp["L0_ffn_g_shared"])
    r["gr_r"] = vec8(inp["L0_ffn_g_routed"])
    r["b2a_r"] = vec8(inp["L0_attn_b2"])
    r["b1a_r"] = rs(np.asarray(inp["L0_attn_b1"], f32).reshape(NH, 128).T)
    r["b1s_r"] = rs(np.asarray(inp["L0_ffn_b1_s"], f32)
                    .reshape(NES, NH, 128).transpose(2, 0, 1)
                    .reshape(128, NES * NH))
    r["b1r_r"] = rs(np.asarray(inp["L0_ffn_b1_r"], f32)
                    .reshape(NER, NH, 128).transpose(2, 0, 1)
                    .reshape(128, NER * NH))
    r["b2r"] = np.asarray(inp["L0_ffn_b2_r"], f32)
    r["b2s0"] = rs(np.asarray(inp["L0_ffn_b2_s"], f32)[0:1])
    r["b2s1"] = rs(np.asarray(inp["L0_ffn_b2_s"], f32)[1:2])
    r["cent_r"] = rs(np.asarray(inp["L0_ffn_centroids"], f32)
                     .reshape(G, 128, NER).transpose(1, 0, 2)
                     .reshape(128, G * NER))
    r["w1"] = np.asarray(inp["L0_attn_w1"], f32)
    r["w2"] = np.asarray(inp["L0_attn_w2"], f32)
    r["W1s"] = np.asarray(inp["L0_ffn_W1_s"], f32)
    r["W2s"] = np.asarray(inp["L0_ffn_W2_s"], f32)
    r["W1r"] = np.asarray(inp["L0_ffn_W1_r"], f32)
    r["W2r"] = np.asarray(inp["L0_ffn_W2_r"], f32)
    r["IDENT"] = np.eye(128, dtype=f32)
    r["ONESBF"] = np.ones((128, 128), dtype=ml_dtypes.bfloat16)
    import math as _m
    dt = np.exp(np.linspace(_m.log(0.001), _m.log(0.1), NS)).astype(f32)
    r["DTREP"] = rs(np.broadcast_to(np.tile(dt, G)[None, :], (128, GN)))
    r["DTRI"] = rs(np.broadcast_to(np.tile((1.0 / dt).astype(f32), G)[None, :],
                                   (128, GN)))
    return r



def kernel(**inputs):
    TC = 2048
    NT = TC // TT
    x = np.asarray(inputs["L0_attn_x"], np.float32)
    shared = _prep_shared(inputs)
    rj = np.ascontiguousarray(np.broadcast_to(
        (TC - 1.0 - np.arange(TC, dtype=np.float32))[None, :], (128, TC)))
    zeros_fm = np.zeros((D, TC), np.float32)

    nc = build_program(NT=NT, NTP=NT)
    in_maps = []
    for c in range(NCORES):
        bb, hf = c // 2, c % 2
        m = dict(shared)
        m["x"] = np.ascontiguousarray(
            x[bb, hf * TC:(hf + 1) * TC, :].T)
        m["xp"] = (np.ascontiguousarray(x[bb, 0:TC, :].T)
                   if hf else zeros_fm)
        m["RJREV"] = rj
        wmtc = np.zeros((16, 128), np.float32)
        wmtc[8:10] = 1.0
        m["WMTC"] = wmtc
        in_maps.append(m)
    res = run_bass_kernel_spmd(nc, in_maps, core_ids=list(range(NCORES)))
    outs = [res.results[c]["out"] for c in range(NCORES)]
    return np.concatenate(outs, axis=0).reshape(BATCH, SEQ, D)


# version 12
# speedup vs baseline: 1.2125x; 1.0937x over previous
import contextlib
import numpy as np
import ml_dtypes

import concourse.bass as bass
import concourse.mybir as mybir
from concourse.bass_utils import run_bass_kernel_spmd

F32 = mybir.dt.float32
BF16 = mybir.dt.bfloat16
AF = mybir.ActivationFunctionType
ALU = mybir.AluOpType
AX = mybir.AxisListType

D = 1024
G = 8
NS = 16
GN = G * NS
HID = 4096
NH = HID // 128
NER, NES = 8, 2
EPS_LN = 1e-5
EPS_RMS = 1e-6
EPS_FFN = 1.1920929e-7
TT = 512
NCORES = 8
BATCH, SEQ = 4, 4096


class Rg:
    __slots__ = ("name", "w", "readers")

    def __init__(self, name):
        self.name = name
        self.w = None
        self.readers = {}


class Bld:
    CENG = ("v", "s", "t", "p", "y")
    NLANE = {"qg": 6, "qs": 6, "qa": 4}

    def __init__(self, nc, es):
        keys = ["v", "s", "t", "p"]
        for q, n in self.NLANE.items():
            keys += [f"{q}{i}" for i in range(n)]
        self.sem = {k: es.enter_context(nc.semaphore("sem_" + k)) for k in keys}
        self.cnt = dict.fromkeys(keys, 0)
        self.qidx = {q: 0 for q in self.NLANE}
        self.wm = {e: {} for e in self.CENG}
        self.plan = {e: [] for e in self.CENG}

    def _waits(self, issuer, reads, writes):
        need = {}
        for r in reads:
            if r.w is not None:
                k, v = r.w
                if v > need.get(k, 0):
                    need[k] = v
        for w in writes:
            if w.w is not None:
                k, v = w.w
                if v > need.get(k, 0):
                    need[k] = v
            for k, v in w.readers.items():
                if v > need.get(k, 0):
                    need[k] = v
        out = []
        wmi = self.wm[issuer]
        for k in sorted(need):
            v = need[k]
            if k == issuer and issuer == "t":
                continue
            if v > wmi.get(k, 0):
                out.append((k, v))
                wmi[k] = v
        return out

    def op(self, eng, emit, reads=(), writes=(), inc=1):
        waits = self._waits(eng, reads, writes)
        if inc:
            self.cnt[eng] += inc
            val = self.cnt[eng]
            self.plan[eng].append((waits, emit, eng, inc))
        else:
            val = self.cnt[eng] + 1
            self.plan[eng].append((waits, emit, None, 0))
        for r in reads:
            if r.readers.get(eng, 0) < val:
                r.readers[eng] = val
        for w in writes:
            w.w = (eng, val)
            w.readers = {}

    def dma(self, q, emit, reads=(), writes=()):
        issuer = {"qg": "p", "qs": "y", "qa": "s"}[q]
        lane = f"{q}{self.qidx[q] % self.NLANE[q]}"
        self.qidx[q] += 1
        waits = self._waits(issuer, reads, writes)
        wmi = self.wm[issuer]
        if self.cnt[lane] > wmi.get(lane, 0):
            waits.append((lane, self.cnt[lane]))
            wmi[lane] = self.cnt[lane]
        self.cnt[lane] += 16
        val = self.cnt[lane]
        self.plan[issuer].append((waits, emit, lane, 16))
        for r in reads:
            if r.readers.get(lane, 0) < val:
                r.readers[lane] = val
        for w in writes:
            w.w = (lane, val)
            w.readers = {}

    def final_wait(self, eng, q):
        for i in range(self.NLANE[q]):
            lane = f"{q}{i}"
            if self.cnt[lane]:
                self.plan[eng].append(
                    ([(lane, self.cnt[lane])], lambda e: e.nop(), None, 0))

    def finalize(self, block, boot_v=0):
        sems = self.sem
        for ename, attr in (("v", "vector"), ("s", "scalar"), ("t", "tensor"),
                            ("p", "gpsimd"), ("y", "sync")):
            items = self.plan[ename]
            pre = [] if (ename == "v" or not boot_v) else [("v", boot_v)]

            def body(eng, items=items, pre=pre):
                for (k, v) in pre:
                    eng.wait_ge(sems[k], v)
                for waits, emit, semk, inc in items:
                    for (k, v) in waits:
                        eng.wait_ge(sems[k], v)
                    ins = emit(eng)
                    if semk is not None:
                        ins.then_inc(sems[semk], inc)
            getattr(block, attr)(body)


def build_program(NT=4, NTP=4, upto=None):
    TC = NT * TT
    TP = NTP * TT
    NTG = TC // 128
    nc = bass.Bass("TRN2", target_bir_lowering=False)

    dr = {}

    def din(name, shape, dt=F32):
        dr[name] = nc.dram_tensor(name, list(shape), dt, kind="ExternalInput")

    din("x", (D, TC))
    din("xp", (D, TP))
    din("w1", (D, HID))
    din("w2", (HID, D))
    din("W1s", (NES, D, HID))
    din("W2s", (NES, HID, D))
    din("W1r", (NER, D, HID))
    din("W2r", (NER, HID, D))
    din("A_r", (128, GN))
    din("B_r", (128, GN))
    din("C_r", (128, GN))
    din("s1_r", (128, G))
    din("s2_r", (128, G))
    din("D_r", (128, G))
    din("gs_r", (128, G))
    din("gr_r", (128, G))
    din("b2a_r", (128, G))
    din("b1a_r", (128, NH))
    din("b1s_r", (128, NES * NH))
    din("b1r_r", (128, NER * NH))
    din("b2r", (NER, D))
    din("b2s0", (1, D))
    din("b2s1", (1, D))
    din("cent_r", (128, G * NER))
    din("IDENT", (128, 128))
    din("ONESBF", (128, 128), BF16)
    din("DTREP", (128, GN))
    din("DTRI", (128, GN))
    din("RJREV", (128, TP))
    din("WMTC", (16, 128))
    out_h = nc.dram_tensor("out", [TC, D], F32, kind="ExternalOutput")
    uhat_d = nc.dram_tensor("uhatd", [128, G * TC], BF16, kind="Internal")
    dbg_h = {}
    if upto is not None:
        for nm, shape in (("dbg_x", [128, G * TC]),
                          ("dbg_m", [128, G * TC // 2]),
                          ("dbg_wtok", [128, NTG * NER]),
                          ("dbg_smid", [128, GN]),
                          ("dbg_otm", [128, NTG * 1024])):
            dbg_h[nm] = nc.dram_tensor(nm, shape, F32, kind="ExternalOutput")

    es = contextlib.ExitStack()
    with es:
        def sbt(name, shape, dt=F32):
            return es.enter_context(nc.sbuf_tensor(name, list(shape), dt))

        XR = sbt("XR", (128, G * TC))
        BIGC = max(4 * G * TT, 8 * TC)
        BIG = sbt("BIG", (128, BIGC))
        MXSC = max(G * TC // 2, TP + 5120 + NTP * 128 + 2 * GN,
                   G * TT + 4096)
        MXS = sbt("MXS", (128, MXSC))
        SSb = sbt("SSb", (128, 4 * TT + 2 * GN + TT))
        STAT = sbt("STAT", (128, 5 * TT + 64))
        WBUF = sbt("WBUF", (128, 4096))
        COEF = sbt("COEF", (128, 5 * GN + 8 * G))
        BIA = sbt("BIA", (128, (1 + NES + NER) * NH))
        B2AUG = sbt("B2AUG", (16, D))
        WTOK = sbt("WTOK", (128, NTG * NER))
        WMT = sbt("WMT", (16, 128))
        IDE = sbt("IDE", (128, 128))
        ONE = sbt("ONE", (128, 128), BF16)

        DTRO = TP + 5120 + NTP * 128
        DTR = MXS[:, DTRO:DTRO + GN]
        DTI = MXS[:, DTRO + GN:DTRO + 2 * GN]
        SCT = STAT[0:8, 3 * TT:4 * TT]

        ps = [es.enter_context(nc.psum_tensor(f"ps{i}", [128, TT], F32))
              for i in range(8)]

        b = Bld(nc, es)
        block = es.enter_context(nc.Block())

        CONSTV = sbt("CONSTV", (128, 16))
        constv_rg = Rg("constv")
        for i, val in enumerate((0.0, 1.0, -0.5, -1.0, EPS_LN, EPS_RMS,
                                 EPS_FFN)):
            b.op("v", (lambda e, ap=CONSTV[:, i:i + 1], v=val:
                       e.memset(ap, v)), writes=[constv_rg])
            nc.const_aps.register(F32, float(val), CONSTV[:, i:i + 1])
        boot_v = b.cnt["v"]

        xr_rg = [Rg(f"xr{i}") for i in range(G * TC // TT)]
        big_rg = [Rg(f"big{i}") for i in range(BIGC // TT)]
        mxs_rg = [Rg(f"mxs{i}") for i in range(MXSC // TT + 1)]

        def rgs(lst, c0, c1):
            return [lst[i] for i in range(c0 // TT, (c1 + TT - 1) // TT)]

        ss_s = [Rg(f"ss_s{i}") for i in range(4)]
        ss_tmp = Rg("ss_tmp")
        ss_scol = [Rg(f"scol{gn}") for gn in range(GN)]
        ss_smid = Rg("smid")
        st_rg = [Rg(f"st{i}") for i in range(5)]
        st_mic = Rg("stmic")
        wb_w1 = [Rg("wbw1a"), Rg("wbw1b")]
        wb_w2 = [Rg("wbw2a"), Rg("wbw2b")]
        coef_rg = Rg("coef")
        bia_rg = Rg("bia")
        b2aug_rg = Rg("b2aug")
        wtok_rg = [Rg(f"wtok{tg}") for tg in range(NTG)]
        wmt_rg = Rg("wmt")
        sct_rg = Rg("sct")
        const_rg = Rg("const")
        ps_rg = [Rg(f"psr{i}") for i in range(8)]
        uhd_rg = Rg("uhatd")
        out_rg = Rg("outh")
        dbg_rg = Rg("dbg")

        def TSc(out, in0, s1, s2, op0, op1=ALU.bypass, acc=None):
            return lambda e: e.tensor_scalar(out, in0, s1, s2, op0, op1,
                                             accum_out=acc)

        def TTo(out, a, c, op):
            return lambda e: e.tensor_tensor(out, a, c, op)

        def STT(out, in0, s, in1, op0, op1, acc=None):
            return lambda e: e.scalar_tensor_tensor(out, in0, s, in1, op0, op1,
                                                    accum_out=acc)

        def CPY(out, in0):
            return lambda e: e.tensor_copy(out, in0)

        def ACT(out, in0, fn, bias=0.0, scale=1.0, acc=None):
            return lambda e: e.activation(out, in0, fn, bias=bias, scale=scale,
                                          accum_out=acc)

        def DMA(out, in0):
            return lambda e: e.dma_start(out=out, in_=in0)

        def MSET(ap, v):
            return lambda e: e.memset(ap, v)

        def bcg(row_ap, n):
            return row_ap.unsqueeze(1).broadcast_to([128, G, n])

        def v3(ap, n):
            return ap.rearrange("p (dg t) -> p dg t", t=n)

        cLogdA = COEF[:, 0:GN]
        cdB = COEF[:, GN:2 * GN]
        cC = COEF[:, 2 * GN:3 * GN]
        cdA = COEF[:, 3 * GN:4 * GN]
        cCB = COEF[:, 4 * GN:5 * GN]
        V8 = 5 * GN

        def cv8(i):
            return COEF[:, V8 + i * G:V8 + (i + 1) * G]

        def srw(i, n=TT):
            return STAT[:, i * TT:i * TT + n]
        MIC = 5 * TT

        def mic(i, n=8):
            return STAT[:, MIC + i * 8:MIC + i * 8 + n]

        b.op("v", MSET(B2AUG[0:16, :], 0.0), writes=[b2aug_rg])
        for dst, src, rg in [
                (IDE[:, :], dr["IDENT"][:, :], const_rg),
                (WMT[0:16, :], dr["WMTC"][:, :], wmt_rg),
                (ONE[:, :], dr["ONESBF"][:, :], const_rg),
                (DTR, dr["DTREP"][:, :], const_rg),
                (DTI, dr["DTRI"][:, :], const_rg),
                (cLogdA, dr["A_r"][:, :], coef_rg),
                (cdB, dr["B_r"][:, :], coef_rg),
                (cC, dr["C_r"][:, :], coef_rg),
                (cv8(0), dr["s1_r"][:, :], coef_rg),
                (cv8(1), dr["s2_r"][:, :], coef_rg),
                (cv8(2), dr["D_r"][:, :], coef_rg),
                (cv8(3), dr["gs_r"][:, :], coef_rg),
                (cv8(4), dr["gr_r"][:, :], coef_rg),
                (cv8(5), dr["b2a_r"][:, :], coef_rg),
                (BIA[:, 0:NH], dr["b1a_r"][:, :], bia_rg),
                (BIA[:, NH:(1 + NES) * NH], dr["b1s_r"][:, :], bia_rg),
                (BIA[:, (1 + NES) * NH:(1 + NES + NER) * NH],
                 dr["b1r_r"][:, :], bia_rg),
                (B2AUG[0:8, :], dr["b2r"][:, :], b2aug_rg),
                (B2AUG[8:9, :], dr["b2s0"][:, :], b2aug_rg),
                (B2AUG[9:10, :], dr["b2s1"][:, :], b2aug_rg)]:
            b.dma("qs", DMA(dst, src), writes=[rg])

        xv = dr["x"][:, :].rearrange("(dg p) t -> p dg t", p=128)
        b.dma("qs", DMA(v3(XR[:, :], TC), xv), writes=list(xr_rg))

        b.op("s", ACT(cdA, cLogdA, AF.Exp), reads=[coef_rg], writes=[coef_rg])
        b.op("v", STT(cLogdA, cdA, -1.0, DTR, ALU.mult, ALU.mult),
             reads=[coef_rg, const_rg], writes=[coef_rg])
        b.op("s", ACT(cdA, cLogdA, AF.Exp), reads=[coef_rg], writes=[coef_rg])
        b.op("v", TSc(cCB, cdA, -1.0, 1.0, ALU.mult, ALU.add),
             reads=[coef_rg], writes=[coef_rg])
        b.op("v", TTo(cdB, cdB, cCB, ALU.mult), reads=[coef_rg],
             writes=[coef_rg])
        b.op("v", STT(cCB, cLogdA, -1.0, DTI, ALU.mult, ALU.mult),
             reads=[coef_rg, const_rg], writes=[coef_rg])
        b.op("v", lambda e: e.reciprocal(cCB, cCB), reads=[coef_rg],
             writes=[coef_rg])
        b.op("v", TTo(cdB, cdB, cCB, ALU.mult), reads=[coef_rg],
             writes=[coef_rg])
        b.op("v", TTo(cCB, cC, cdB, ALU.mult), reads=[coef_rg],
             writes=[coef_rg])
        for g in range(G):
            b.op("v", TSc(cCB[:, g * NS:(g + 1) * NS],
                          cCB[:, g * NS:(g + 1) * NS],
                          cv8(0)[:, g:g + 1], None, ALU.mult),
                 reads=[coef_rg], writes=[coef_rg])
        b.op("v", TTo(cv8(6), cv8(2), cv8(0), ALU.mult),
             reads=[coef_rg], writes=[coef_rg])


        def stats(xbf_ap, xbf_rgs, eps2=None):
            def e_s1(e):
                last = None
                for dg in range(G):
                    last = e.matmul(ps[6][:, 0:TT], ONE[:, :],
                                    xbf_ap[:, dg * TT:(dg + 1) * TT],
                                    start=(dg == 0), stop=(dg == G - 1))
                return last
            b.op("t", e_s1, reads=xbf_rgs + [const_rg], writes=[ps_rg[6]])
            b.op("s", ACT(xbf_ap, xbf_ap, AF.Square), reads=xbf_rgs,
                 writes=xbf_rgs)

            def e_s2(e):
                last = None
                for dg in range(G):
                    last = e.matmul(ps[7][:, 0:TT], ONE[:, :],
                                    xbf_ap[:, dg * TT:(dg + 1) * TT],
                                    start=(dg == 0), stop=(dg == G - 1))
                return last
            b.op("t", e_s2, reads=xbf_rgs + [const_rg], writes=[ps_rg[7]])
            var, rstd, mu, ntm = srw(0), srw(1), srw(2), srw(3)
            rv, rr, rm, rn = st_rg[0], st_rg[1], st_rg[2], st_rg[3]
            b.op("v", TSc(mu, ps[6][:, 0:TT], 1.0 / D, None, ALU.mult),
                 reads=[ps_rg[6]], writes=[rm])
            b.op("v", TTo(var, mu, mu, ALU.mult), reads=[rm], writes=[rv])
            b.op("v", STT(var, ps[7][:, 0:TT], 1.0 / D, var, ALU.mult,
                          ALU.subtract), reads=[ps_rg[7], rv], writes=[rv])
            if eps2 is not None:
                inv2 = srw(4)
                ri = st_rg[4]
                b.op("v", TSc(inv2, var, EPS_LN, None, ALU.add),
                     reads=[rv], writes=[ri])
                b.op("v", lambda e, o=inv2, i=inv2: e.reciprocal(o, i),
                     reads=[ri], writes=[ri])
                b.op("v", TTo(inv2, inv2, var, ALU.mult),
                     reads=[ri, rv], writes=[ri])
                b.op("s", ACT(inv2, inv2, AF.Ln, bias=eps2),
                     reads=[ri], writes=[ri])
                b.op("s", ACT(inv2, inv2, AF.Exp, scale=-0.5),
                     reads=[ri], writes=[ri])
            b.op("v", TSc(var, var, EPS_LN, None, ALU.add),
                 reads=[rv], writes=[rv])
            b.op("s", ACT(rstd, var, AF.Ln), reads=[rv], writes=[rr])
            b.op("s", ACT(rstd, rstd, AF.Exp, scale=-0.5),
                 reads=[rr], writes=[rr])
            b.op("v", TTo(ntm, rstd, rstd, ALU.mult), reads=[rr], writes=[rn])
            b.op("v", TTo(ntm, ntm, var, ALU.mult), reads=[rn, rv],
                 writes=[rn])
            b.op("v", TSc(ntm, ntm, -0.5, 1.5, ALU.mult, ALU.add),
                 reads=[rn], writes=[rn])
            b.op("v", TTo(rstd, rstd, ntm, ALU.mult),
                 reads=[rr, rn], writes=[rr])
            return var, rstd, mu

        def stats_sq_only(xbf_ap, xbf_rgs, eps):
            def e_s2(e):
                last = None
                for dg in range(G):
                    last = e.matmul(ps[7][:, 0:TT], ONE[:, :],
                                    xbf_ap[:, dg * TT:(dg + 1) * TT],
                                    start=(dg == 0), stop=(dg == G - 1))
                return last
            b.op("t", e_s2, reads=xbf_rgs + [const_rg], writes=[ps_rg[7]])
            inv = srw(4)
            ri = st_rg[4]
            b.op("v", TSc(inv, ps[7][:, 0:TT], 1.0 / D, None, ALU.mult),
                 reads=[ps_rg[7]], writes=[ri])
            b.op("s", ACT(inv, inv, AF.Ln, bias=eps), reads=[ri], writes=[ri])
            b.op("s", ACT(inv, inv, AF.Exp, scale=-0.5),
                 reads=[ri], writes=[ri])
            return inv, ri

        w1slot = [(STAT[:, 0:1024].bitcast(BF16), [st_rg[0], st_rg[1]]),
                  (STAT[:, 1024:2048].bitcast(BF16), [st_rg[2], st_rg[3]])]
        w2slot = [(WBUF[:, i * 1024:(i + 1) * 1024].bitcast(BF16),
                   [[wb_w2[0], wb_w2[1], wb_w1[0], wb_w1[1]][i]])
                  for i in range(4)]
        w2slot2 = [(WBUF[:, 0:2048].bitcast(BF16), [wb_w2[0], wb_w2[1]]),
                   (WBUF[:, 2048:4096].bitcast(BF16), [wb_w1[0], wb_w1[1]])]


        smid_ap = SSb[:, 4 * TT + GN:4 * TT + 2 * GN]
        b.op("v", MSET(smid_ap, 0.0), writes=[ss_smid])
        rjr = MXS[:, 0:TP]
        rj_rg = rgs(mxs_rg, 0, TP)
        b.dma("qs", DMA(rjr, dr["RJREV"][:, :]), writes=rj_rg)
        xpv = dr["xp"][:, :].rearrange("(dg p) t -> p dg t", p=128)

        xbfo = TP
        expwo = TP + 2048
        NEXPW = 4
        dumpo = expwo + NEXPW * TT
        smpo = dumpo + TT
        xbf_rg_p1 = rgs(mxs_rg, xbfo, xbfo + 2048)
        expw_rg = [rgs(mxs_rg, expwo + i * TT, expwo + (i + 1) * TT)[0]
                   for i in range(NEXPW)]
        dump_rg = rgs(mxs_rg, dumpo, dumpo + TT)
        smp_rg = [rgs(mxs_rg, smpo + i * 128, smpo + (i + 1) * 128)[0]
                  for i in range(NTP)]

        for pt in range(NTP):
            sl = pt % 2
            xpo = G * TT * (2 + sl)
            xp_ap = BIG[:, xpo:xpo + G * TT]
            xp_rgs = rgs(big_rg, xpo, xpo + G * TT)
            b.dma("qs", DMA(v3(xp_ap, TT), xpv[:, :, pt * TT:(pt + 1) * TT]),
                  writes=xp_rgs)
            xbf = MXS[:, xbfo:xbfo + 2048].bitcast(BF16)
            b.dma("qg", DMA(xbf.rearrange("p (dg t) -> p dg t", t=TT),
                            xpv[:, :, pt * TT:(pt + 1) * TT]),
                  writes=xbf_rg_p1)
            var, rstd, mu = stats(xbf, xbf_rg_p1, eps2=EPS_RMS)
            inv2 = srw(4)
            comb, rc = srw(3), st_rg[3]
            mucomb, rmc = srw(0), st_rg[0]
            b.op("v", TTo(comb, rstd, inv2, ALU.mult),
                 reads=[st_rg[1], st_rg[4]], writes=[rc])
            b.op("v", TTo(mucomb, mu, comb, ALU.mult),
                 reads=[st_rg[2], rc], writes=[rmc])
            upo = G * TT * sl
            up_ap = BIG[:, upo:upo + G * TT]
            up_rgs = rgs(big_rg, upo, upo + G * TT)
            b.op("v", TTo(v3(up_ap, TT), v3(xp_ap, TT), bcg(comb, TT),
                          ALU.mult), reads=xp_rgs + [rc], writes=up_rgs)
            b.op("v", TTo(v3(up_ap, TT), v3(up_ap, TT), bcg(mucomb, TT),
                          ALU.subtract), reads=up_rgs + [rmc], writes=up_rgs)
            smp_ap = MXS[:, smpo + pt * 128:smpo + (pt + 1) * 128]
            for gn in range(GN):
                g = gn // NS
                esl = gn % NEXPW
                ew = MXS[:, expwo + esl * TT:expwo + (esl + 1) * TT]
                b.op("s", ACT(ew, rjr[:, pt * TT:(pt + 1) * TT], AF.Exp,
                              scale=cLogdA[:, gn:gn + 1]),
                     reads=rj_rg + [coef_rg], writes=[expw_rg[esl]])
                b.op("v", STT(MXS[:, dumpo:dumpo + TT], ew, 1.0,
                              up_ap[:, g * TT:(g + 1) * TT], ALU.mult,
                              ALU.mult, acc=smp_ap[:, gn:gn + 1]),
                     reads=[expw_rg[esl], up_rgs[g]],
                     writes=dump_rg + [smp_rg[pt]])
        for pt in range(NTP):
            smp_ap = MXS[:, smpo + pt * 128:smpo + (pt + 1) * 128]
            b.op("v", TTo(smid_ap, smid_ap, smp_ap, ALU.add),
                 reads=[smp_rg[pt], ss_smid], writes=[ss_smid])

        us_rgs = rgs(big_rg, 0, G * 2 * TT)
        y_ap = BIG[:, 2 * G * TT:2 * G * TT + G * TT]
        y_rgs = rgs(big_rg, 2 * G * TT, 3 * G * TT)
        yp_ap = BIG[:, 3 * G * TT:4 * G * TT]
        yp_rgs = rgs(big_rg, 3 * G * TT, 4 * G * TT)
        m_bf = MXS[:, 0:G * TC // 2].bitcast(BF16)

        def m_rgs(dg, t):
            return rgs(mxs_rg, dg * TC // 2 + t * TT // 2,
                       dg * TC // 2 + (t + 1) * TT // 2)

        us4 = BIG[:, 0:G * 2 * TT].rearrange("p (dg h t) -> p dg h t",
                                             h=2, t=TT)

        for tp_ in range(NT // 2):
            for half in range(2):
                t = 2 * tp_ + half
                sl = half
                xbf = WBUF[:, sl * 2048:(sl + 1) * 2048].bitcast(BF16)
                xbf_rgs = ([wb_w2[0], wb_w2[1]] if sl == 0
                           else [wb_w1[0], wb_w1[1]])
                b.dma("qg", DMA(xbf.rearrange("p (dg t) -> p dg t", t=TT),
                                xv[:, :, t * TT:(t + 1) * TT]),
                      writes=xbf_rgs)
                var, rstd, mu = stats(xbf, xbf_rgs, eps2=EPS_RMS)
                inv2, ri2 = srw(4), st_rg[4]
                comb, rc = srw(3), st_rg[3]
                b.op("v", TTo(comb, rstd, inv2, ALU.mult),
                     reads=[st_rg[1], ri2], writes=[rc])
                mucomb, rmc = srw(1), st_rg[1]
                b.op("v", TTo(mucomb, mu, comb, ALU.mult),
                     reads=[st_rg[2], rc], writes=[rmc])
                xt = v3(XR[:, :], TC)[:, :, t * TT:(t + 1) * TT]
                xt_rgs = [xr_rg[(dg * TC + t * TT) // TT] for dg in range(G)]
                uhalf = us4[:, :, half, :]
                b.op("v", TTo(uhalf, xt, bcg(comb, TT), ALU.mult),
                     reads=xt_rgs + [rc], writes=us_rgs)
                b.op("v", TTo(uhalf, uhalf, bcg(mucomb, TT),
                              ALU.subtract), reads=us_rgs + [rmc],
                     writes=us_rgs)
                b.op("v", lambda e, h=half, i=inv2:
                     e.reciprocal(ps[4 + h][:, 0:TT], i),
                     reads=[ri2], writes=[ps_rg[4 + half]])
            b.op("v", MSET(y_ap, 0.0), writes=y_rgs)
            b.op("p", lambda e: e.memset(yp_ap, 0.0), writes=yp_rgs)
            tmp_p = SSb[:, 4 * TT + 2 * GN:5 * TT + 2 * GN]
            for gn in range(GN):
                g = gn // NS
                ssl = gn % 2
                s_ap = SSb[:, ssl * 2 * TT:(ssl + 1) * 2 * TT]
                if tp_ == 0:
                    init = smid_ap[:, gn:gn + 1]
                    init_rg = ss_smid
                else:
                    init = SSb[:, 4 * TT + gn:4 * TT + gn + 1]
                    init_rg = ss_scol[gn]
                b.op("v", lambda e, o=s_ap, gn=gn, g=g, ini=init:
                     e.tensor_tensor_scan(
                         o, cdA[:, gn:gn + 1].broadcast_to([128, 2 * TT]),
                         BIG[:, g * 2 * TT:(g + 1) * 2 * TT], ini,
                         ALU.mult, ALU.add),
                     reads=[coef_rg, init_rg, us_rgs[2 * g],
                            us_rgs[2 * g + 1]],
                     writes=[ss_s[ssl]])
                b.op("v", STT(y_ap[:, g * TT:(g + 1) * TT], s_ap[:, 0:TT],
                              cCB[:, gn:gn + 1],
                              y_ap[:, g * TT:(g + 1) * TT],
                              ALU.mult, ALU.add),
                     reads=[ss_s[ssl], coef_rg, y_rgs[g]],
                     writes=[y_rgs[g]])
                b.op("p", TTo(tmp_p, s_ap[:, TT:2 * TT],
                              cCB[:, gn:gn + 1].broadcast_to([128, TT]),
                              ALU.mult),
                     reads=[ss_s[ssl], coef_rg], writes=[ss_tmp])
                b.op("p", TTo(yp_ap[:, g * TT:(g + 1) * TT],
                              yp_ap[:, g * TT:(g + 1) * TT], tmp_p,
                              ALU.add),
                     reads=[ss_tmp, yp_rgs[g]], writes=[yp_rgs[g]])
                if tp_ < NT // 2 - 1:
                    b.op("p", CPY(SSb[:, 4 * TT + gn:4 * TT + gn + 1],
                                  s_ap[:, 2 * TT - 1:2 * TT]),
                         reads=[ss_s[ssl]], writes=[ss_scol[gn]])
            for half in range(2):
                t = 2 * tp_ + half
                yb_ap = y_ap if half == 0 else yp_ap
                yb_rgs = y_rgs if half == 0 else yp_rgs
                uhalf = us4[:, :, half, :]
                for dg in range(G):
                    b.op("v", STT(yb_ap[:, dg * TT:(dg + 1) * TT],
                                  BIG[:, dg * 2 * TT + half * TT:
                                      dg * 2 * TT + (half + 1) * TT],
                                  cv8(6)[:, dg:dg + 1],
                                  yb_ap[:, dg * TT:(dg + 1) * TT],
                                  ALU.mult, ALU.add),
                         reads=[us_rgs[2 * dg + half],
                                coef_rg, yb_rgs[dg]],
                         writes=[yb_rgs[dg]])
                rc2row = ps[4 + half][:, 0:TT]
                b.op("v", TTo(uhalf, uhalf, bcg(rc2row, TT), ALU.mult),
                     reads=us_rgs + [ps_rg[4 + half]], writes=us_rgs)
                b.op("v", TTo(v3(yb_ap, TT), v3(yb_ap, TT), uhalf, ALU.add),
                     reads=yb_rgs + us_rgs, writes=yb_rgs)
                xt = v3(XR[:, :], TC)[:, :, t * TT:(t + 1) * TT]
                xt_rgs = [xr_rg[(dg * TC + t * TT) // TT] for dg in range(G)]
                sl = half
                xbf2 = WBUF[:, sl * 2048:(sl + 1) * 2048].bitcast(BF16)
                xbf_rgs = ([wb_w2[0], wb_w2[1]] if sl == 0
                           else [wb_w1[0], wb_w1[1]])
                b.op("s", ACT(xbf2, yb_ap, AF.Square), reads=yb_rgs,
                     writes=xbf_rgs)
                inv3, ri3 = stats_sq_only(xbf2, xbf_rgs, EPS_RMS)
                b.op("v", TTo(uhalf, v3(yb_ap, TT), bcg(inv3, TT), ALU.mult),
                     reads=yb_rgs + [ri3], writes=us_rgs)
                for dg in range(G):
                    b.op("s", ACT(m_bf[:, dg * TC + t * TT:
                                       dg * TC + (t + 1) * TT],
                                  BIG[:, dg * 2 * TT + half * TT:
                                      dg * 2 * TT + (half + 1) * TT],
                                  AF.Copy, scale=cv8(1)[:, dg:dg + 1]),
                         reads=[us_rgs[2 * dg + half], coef_rg],
                         writes=m_rgs(dg, t))
                b.op("v", TTo(xt, xt, v3(yb_ap, TT), ALU.add),
                     reads=xt_rgs + yb_rgs, writes=xt_rgs)

        def dump(which):
            if "x" in which:
                b.dma("qs", DMA(dbg_h["dbg_x"][:, :], XR[:, :]),
                      reads=list(xr_rg), writes=[dbg_rg])
            if "m" in which:
                b.dma("qs", DMA(dbg_h["dbg_m"][:, :],
                                MXS[:, 0:G * TC // 2]),
                      reads=list(mxs_rg[:G * TC // 2 // TT]), writes=[dbg_rg])
            if "smid" in which:
                b.dma("qs", DMA(dbg_h["dbg_smid"][:, :], smid_ap),
                      reads=[ss_smid], writes=[dbg_rg])
            if "wtok" in which:
                b.dma("qs", DMA(dbg_h["dbg_wtok"][:, :], WTOK[:, :]),
                      reads=list(wtok_rg), writes=[dbg_rg])
            if "otm" in which:
                b.dma("qs", DMA(dbg_h["dbg_otm"][:, :], BIG[:, 0:NTG * 1024]),
                      reads=rgs(big_rg, 0, NTG * 1024), writes=[dbg_rg])
            b.final_wait("y", "qs")

        if upto == "P2":
            dump(("x", "m", "smid"))
            b.finalize(block, boot_v)
            return nc

        NQ = 4
        QH = HID // NQ
        bigbf = BIG[:, :].bitcast(BF16)
        m_all = rgs(mxs_rg, 0, G * TC // 2)

        def h3_ap(slab, hsub, tt):
            c = slab * (G * TC) + hsub * TC + tt * TT
            return bigbf[:, c:c + TT], rgs(big_rg, c // 2, (c + TT) // 2)

        for q in range(NQ):
            slab = q % 2
            for hc in range(QH // 256):
                wsl = (q * (QH // 256) + hc) % 2
                w1c, w1rgs = w1slot[wsl]
                h0 = q * QH + hc * 256
                src = dr["w1"][:, h0:h0 + 256].rearrange(
                    "(dg p) h -> p dg h", p=128)
                b.dma("qg", DMA(w1c.rearrange("p (dg h) -> p dg h", h=256),
                                src), writes=w1rgs)
                for hs in range(2):
                    hsub = hc * 2 + hs
                    for tt in range(NT):
                        pp = ps[(hsub * NT + tt) % 3]
                        pr = ps_rg[(hsub * NT + tt) % 3]

                        def e_mm1(e, pp=pp, w1c=w1c, hs=hs, tt=tt):
                            last = None
                            for dg in range(G):
                                last = e.matmul(
                                    pp[:, 0:TT],
                                    w1c[:, dg * 256 + hs * 128:
                                        dg * 256 + (hs + 1) * 128],
                                    m_bf[:, dg * TC + tt * TT:
                                         dg * TC + (tt + 1) * TT],
                                    start=(dg == 0), stop=(dg == G - 1))
                            return last
                        b.op("t", e_mm1, reads=w1rgs + m_all,
                             writes=[pr])
                        hap, h_rgs = h3_ap(slab, hsub, tt)
                        b.op("s", ACT(hap, pp[:, 0:TT], AF.Gelu,
                                      bias=BIA[:, q * 8 + hsub:
                                               q * 8 + hsub + 1]),
                             reads=[pr, bia_rg], writes=h_rgs)
            for tt in range(NT):
                for hsub in range(8):
                    psl = (hsub // 4) % 2
                    if hsub % 4 == 0:
                        w2p, w2rgs = w2slot2[psl]
                        hrow = q * QH + hsub * 128
                        b.dma("qg", DMA(
                            w2p.rearrange("p (h2 d) -> p h2 d", d=D),
                            dr["w2"][hrow:hrow + 512, :].rearrange(
                                "(h2 p) d -> p h2 d", p=128)),
                            writes=w2rgs)
                    w2p, w2rgs = w2slot2[psl]
                    w2t = w2p[:, (hsub % 4) * D:(hsub % 4 + 1) * D]
                    hap, h_rgs = h3_ap(slab, hsub, tt)

                    def e_mm2(e, w2t=w2t, hap=hap, hsub=hsub):
                        last = None
                        for dmg in range(G):
                            last = e.matmul(ps[dmg][:, 0:TT],
                                            w2t[:, dmg * 128:(dmg + 1) * 128],
                                            hap, start=(hsub == 0),
                                            stop=(hsub == 7))
                        return last
                    b.op("t", e_mm2, reads=w2rgs + h_rgs,
                         writes=[ps_rg[dmg] for dmg in range(G)])
                for dmg in range(G):
                    xap = XR[:, dmg * TC + tt * TT:dmg * TC + (tt + 1) * TT]
                    xrg = xr_rg[(dmg * TC + tt * TT) // TT]
                    if q < NQ - 1:
                        b.op("v", TTo(xap, xap, ps[dmg][:, 0:TT], ALU.add),
                             reads=[ps_rg[dmg], xrg], writes=[xrg])
                    else:
                        b.op("v", STT(xap, ps[dmg][:, 0:TT],
                                      cv8(5)[:, dmg:dmg + 1], xap,
                                      ALU.add, ALU.add),
                             reads=[ps_rg[dmg], coef_rg, xrg], writes=[xrg])

        if upto == "P3":
            dump(("x", "m"))
            b.finalize(block, boot_v)
            return nc

        uo = 0
        u_ap = MXS[:, uo:uo + G * TT]
        u_rgs = rgs(mxs_rg, uo, uo + G * TT)
        uh_o = G * TT
        uhs_rg = [rgs(mxs_rg, uh_o, uh_o + 2048),
                  rgs(mxs_rg, uh_o + 2048, uh_o + 4096)]
        cent_sb = WBUF[:, 0:64]
        b.dma("qs", DMA(cent_sb, dr["cent_r"][:, :]), writes=[wb_w2[0]])
        xbf4 = WBUF[:, 64:64 + 2048].bitcast(BF16)
        xbf4_rgs = [wb_w2[0], wb_w2[1], wb_w1[0], wb_w1[1]]

        for t in range(NT):
            xt = v3(XR[:, :], TC)[:, :, t * TT:(t + 1) * TT]
            xt_rgs = [xr_rg[(dg * TC + t * TT) // TT] for dg in range(G)]
            b.op("v", CPY(xbf4.rearrange("p (dg t) -> p dg t", t=TT), xt),
                 reads=xt_rgs, writes=xbf4_rgs)
            var, rstd, mu = stats(xbf4, xbf4_rgs, eps2=EPS_FFN)
            inv4, ri4 = srw(4), st_rg[4]
            mucomb, rmc = srw(3), st_rg[3]
            b.op("v", TTo(mucomb, mu, rstd, ALU.mult),
                 reads=[st_rg[2], st_rg[1]], writes=[rmc])
            b.op("v", TTo(v3(u_ap, TT), xt, bcg(rstd, TT), ALU.mult),
                 reads=xt_rgs + [st_rg[1]], writes=u_rgs)
            b.op("v", TTo(v3(u_ap, TT), v3(u_ap, TT), bcg(mucomb, TT),
                          ALU.subtract), reads=u_rgs + [rmc], writes=u_rgs)

            def e_sc(e):
                last = None
                for dg in range(G):
                    last = e.matmul(ps[3][0:8, 0:TT],
                                    cent_sb[:, dg * 8:(dg + 1) * 8],
                                    u_ap[:, dg * TT:(dg + 1) * TT],
                                    start=(dg == 0), stop=(dg == G - 1))
                return last
            b.op("t", e_sc, reads=u_rgs + [wb_w2[0]], writes=[ps_rg[3]])
            b.op("v", CPY(SCT[0:8, 0:TT], ps[3][0:8, 0:TT]),
                 reads=[ps_rg[3]], writes=[st_rg[3]])
            for tgl in range(TT // 128):
                tg = t * (TT // 128) + tgl
                b.op("t", lambda e, tgl=tgl: e.transpose(
                    ps[3][:, 0:8], SCT[0:8, tgl * 128:(tgl + 1) * 128],
                    IDE[0:8, 0:8]), reads=[st_rg[3], const_rg],
                    writes=[ps_rg[3]])
                lg = mic(0)
                b.op("v", CPY(lg, ps[3][:, 0:8]), reads=[ps_rg[3]],
                     writes=[st_mic])
                m1 = mic(4, 1)
                b.op("v", lambda e, o=m1, i=lg: e.reduce_max(o, i, AX.X),
                     reads=[st_mic], writes=[st_mic])
                mk1 = mic(1)
                b.op("v", TSc(mk1, lg, m1, None, ALU.is_ge),
                     reads=[st_mic], writes=[st_mic])
                t2 = mic(2)
                b.op("v", STT(t2, mk1, -1e30, lg, ALU.mult, ALU.add),
                     reads=[st_mic], writes=[st_mic])
                m2 = mic(5, 1)
                b.op("v", lambda e, o=m2, i=t2: e.reduce_max(o, i, AX.X),
                     reads=[st_mic], writes=[st_mic])
                mk = mic(1)
                b.op("v", TSc(mk, lg, m2, None, ALU.is_ge),
                     reads=[st_mic], writes=[st_mic])
                sg = mic(3)
                b.op("s", ACT(sg, lg, AF.Exp, scale=-1.0),
                     reads=[st_mic], writes=[st_mic])
                b.op("v", TSc(sg, sg, 1.0, None, ALU.add),
                     reads=[st_mic], writes=[st_mic])
                b.op("v", lambda e, o=sg, i=sg: e.reciprocal(o, i),
                     reads=[st_mic], writes=[st_mic])
                dn = mic(6, 1)
                b.op("v", lambda e, o=dn, i=sg: e.reduce_sum(o, i, AX.X),
                     reads=[st_mic], writes=[st_mic])
                wv = mic(2)
                b.op("v", TTo(wv, sg, mk, ALU.mult), reads=[st_mic],
                     writes=[st_mic])
                b.op("v", lambda e, o=dn, i=dn: e.reciprocal(o, i),
                     reads=[st_mic], writes=[st_mic])
                b.op("v", TSc(WTOK[:, tg * 8:(tg + 1) * 8], wv, dn, None,
                              ALU.mult), reads=[st_mic],
                     writes=[wtok_rg[tg]])
            usl = t % 2
            uh_ap = MXS[:, uh_o + usl * 2048:uh_o + (usl + 1) * 2048]
            uh_bf = uh_ap.bitcast(BF16)
            b.op("v", TTo(uh_bf.rearrange("p (dg t) -> p dg t", t=TT),
                          v3(u_ap, TT), bcg(inv4, TT), ALU.mult),
                 reads=u_rgs + [ri4], writes=uhs_rg[usl])
            b.dma("qg", DMA(
                dr_uhat_view(uhat_d, t, TC), uh_bf.rearrange(
                    "p (dg t) -> p dg t", t=TT)),
                reads=uhs_rg[usl], writes=[uhd_rg])
            b.op("v", TTo(v3(u_ap, TT), v3(u_ap, TT), xt, ALU.add),
                 reads=u_rgs + xt_rgs, writes=u_rgs)
            for tgl in range(TT // 128):
                tg = t * (TT // 128) + tgl
                for dg in range(G):
                    pt_ = ps[4 + dg // 4]
                    b.op("t", lambda e, dg=dg, tgl=tgl, pt_=pt_: e.transpose(
                        pt_[:, (dg % 4) * 128:(dg % 4 + 1) * 128],
                        u_ap[:, dg * TT + tgl * 128:dg * TT + (tgl + 1) * 128],
                        IDE[:, :]),
                        reads=u_rgs + [const_rg],
                        writes=[ps_rg[4 + dg // 4]],
                        inc=(1 if dg % 4 == 3 else 0))
                for hh in range(2):
                    b.op("v", CPY(BIG[:, tg * 1024 + hh * 512:
                                       tg * 1024 + (hh + 1) * 512],
                                  ps[4 + hh][:, 0:TT]),
                         reads=[ps_rg[4 + hh]],
                         writes=rgs(big_rg, tg * 1024 + hh * 512,
                                    tg * 1024 + (hh + 1) * 512))

        if upto == "P4":
            dump(("wtok", "otm"))
            b.finalize(block, boot_v)
            return nc

        xn_bf = MXS[:, 0:G * TC // 2].bitcast(BF16)
        xn_all = rgs(mxs_rg, 0, G * TC // 2)
        xrbf = XR[:, :].bitcast(BF16)

        n_slab5 = max(1, (G * TC) // 8192)

        def h5(slab, hsub):
            c = slab * 16384 + hsub * TT
            return (xrbf[:, c:c + TT],
                    rgs(xr_rg, c // 2, (c + TT) // 2))

        def load_xn(gvec_idx):
            for half in range(2):
                b.dma("qg", DMA(
                    xn_bf[:, half * (G * TC // 2):(half + 1) * (G * TC // 2)],
                    uhat_d[:, half * (G * TC // 2):(half + 1) * (G * TC // 2)]),
                    reads=[uhd_rg], writes=xn_all)
            for dg in range(G):
                b.op("v", TSc(xn_bf[:, dg * TC:(dg + 1) * TC],
                              xn_bf[:, dg * TC:(dg + 1) * TC],
                              cv8(gvec_idx)[:, dg:dg + 1], None, ALU.mult),
                     reads=xn_all + [coef_rg], writes=xn_all)

        def expert(eidx, w1_src, w2_srch, b1col, wcol):
            PSZ = min(2, n_slab5)
            for pq in range(NT // PSZ):
                tqs = [pq * PSZ + i for i in range(PSZ)]
                for hc in range(HID // 256):
                    wsl = hc % 2
                    w1c, w1rgs = w1slot[wsl]
                    b.dma("qg", DMA(
                        w1c.rearrange("p (dg h) -> p dg h", h=256),
                        w1_src(hc * 256, 256)), writes=w1rgs)
                    for hs in range(2):
                        hsub = hc * 2 + hs
                        for tq in tqs:
                            pidx = hsub * PSZ + (tq % PSZ)
                            pp = ps[pidx % 3]
                            pr = ps_rg[pidx % 3]

                            def e_mm1(e, pp=pp, w1c=w1c, hs=hs, tq=tq):
                                last = None
                                for dg in range(G):
                                    last = e.matmul(
                                        pp[:, 0:TT],
                                        w1c[:, dg * 256 + hs * 128:
                                            dg * 256 + (hs + 1) * 128],
                                        xn_bf[:, dg * TC + tq * TT:
                                              dg * TC + (tq + 1) * TT],
                                        start=(dg == 0), stop=(dg == G - 1))
                                return last
                            b.op("t", e_mm1, reads=w1rgs + xn_all,
                                 writes=[pr])
                            hap, h_rgs = h5(tq % n_slab5, hsub)
                            b.op("s", ACT(hap, pp[:, 0:TT], AF.Gelu,
                                          bias=BIA[:, b1col + hsub:
                                                   b1col + hsub + 1]),
                                 reads=[pr, bia_rg], writes=h_rgs)
                mm2t_pair(tqs, w2_srch, wcol)

        def mm2t_pair(tqs, w2_srch, wcol):
            for dmh in range(2):
                for hsub in range(NH):
                    psl = (hsub // 4) % 4
                    if hsub % 4 == 0:
                        w2p, w2rgs = w2slot[psl]
                        b.dma("qg", DMA(
                            w2p.rearrange("p (h2 d) -> p h2 d", d=512),
                            w2_srch(hsub * 128, dmh)), writes=w2rgs)
                    w2p, w2rgs = w2slot[psl]
                    w2t = w2p[:, (hsub % 4) * 512:(hsub % 4 + 1) * 512]
                    hrds = []
                    haps = []
                    for tq in tqs:
                        hap, h_rgs = h5(tq % n_slab5, hsub)
                        haps.append(hap)
                        hrds += h_rgs

                    def e_mm2(e, w2t=w2t, haps=haps, hsub=hsub, tqs=tqs):
                        last = None
                        for tgl in range(4):
                            for ti in range(len(tqs)):
                                last = e.matmul(
                                    ps[tgl * 2 + ti][:, 0:TT],
                                    haps[ti][:, tgl * 128:(tgl + 1) * 128],
                                    w2t, start=(hsub == 0),
                                    stop=(hsub == NH - 1))
                        return last
                    b.op("t", e_mm2, reads=w2rgs + hrds,
                         writes=[ps_rg[i] for i in range(8)])
                for ti, tq in enumerate(tqs):
                    for tgl in range(4):
                        tg = tq * 4 + tgl
                        oap = BIG[:, tg * 1024 + dmh * 512:
                                  tg * 1024 + (dmh + 1) * 512]
                        orgs = rgs(big_rg, tg * 1024 + dmh * 512,
                                   tg * 1024 + (dmh + 1) * 512)
                        scl = (1.0 if wcol is None else
                               WTOK[:, tg * 8 + wcol:tg * 8 + wcol + 1])
                        rds = [ps_rg[tgl * 2 + ti]] + orgs
                        if wcol is not None:
                            rds.append(wtok_rg[tg])
                        b.op("v", STT(oap, ps[tgl * 2 + ti][:, 0:TT], scl,
                                      oap, ALU.mult, ALU.add),
                             reads=rds, writes=orgs)

        load_xn(3)
        for e_i in range(NES):
            expert(e_i,
                   lambda h0, n, e_i=e_i: dr["W1s"][e_i, :, h0:h0 + n]
                   .rearrange("(dg p) h -> p dg h", p=128),
                   lambda hr, dmh, e_i=e_i:
                   dr["W2s"][e_i, hr:hr + 512, dmh * 512:(dmh + 1) * 512]
                   .rearrange("(h2 p) d -> p h2 d", p=128),
                   NH + e_i * NH, None)
        load_xn(4)
        for e_i in range(NER):
            expert(e_i,
                   lambda h0, n, e_i=e_i: dr["W1r"][e_i, :, h0:h0 + n]
                   .rearrange("(dg p) h -> p dg h", p=128),
                   lambda hr, dmh, e_i=e_i:
                   dr["W2r"][e_i, hr:hr + 512, dmh * 512:(dmh + 1) * 512]
                   .rearrange("(h2 p) d -> p h2 d", p=128),
                   (1 + NES) * NH + e_i * NH, e_i)

        for tg in range(NTG):
            b.op("t", lambda e, tg=tg: e.transpose(
                ps[3][:, 0:8]
                if False else ps[3][0:8, 0:128],
                WTOK[:, tg * 8:(tg + 1) * 8], IDE[:, :]),
                reads=[wtok_rg[tg], const_rg], writes=[ps_rg[3]])
            b.op("v", CPY(WMT[0:8, :], ps[3][0:8, 0:128]),
                 reads=[ps_rg[3]], writes=[wmt_rg])
            for dmh in range(2):
                b.op("t", lambda e, dmh=dmh: e.matmul(
                    ps[4 + dmh][:, 0:TT], WMT[0:16, :],
                    B2AUG[0:16, dmh * 512:(dmh + 1) * 512],
                    start=True, stop=True),
                    reads=[wmt_rg, b2aug_rg], writes=[ps_rg[4 + dmh]])
            for dmh in range(2):
                oap = BIG[:, tg * 1024 + dmh * 512:
                          tg * 1024 + (dmh + 1) * 512]
                orgs = rgs(big_rg, tg * 1024 + dmh * 512,
                           tg * 1024 + (dmh + 1) * 512)
                b.op("v", TTo(oap, oap, ps[4 + dmh][:, 0:TT], ALU.add),
                     reads=[ps_rg[4 + dmh]] + orgs, writes=orgs)
            b.dma("qa", DMA(out_h[tg * 128:(tg + 1) * 128, :],
                            BIG[:, tg * 1024:(tg + 1) * 1024]),
                  reads=rgs(big_rg, tg * 1024, (tg + 1) * 1024),
                  writes=[out_rg])
        b.final_wait("y", "qa")
        b.finalize(block, boot_v)
        return nc


def dr_uhat_view(uhat_d, t, TC):
    return uhat_d[:, :].rearrange("p (dg t) -> p dg t", t=TC)[
        :, :, t * TT:(t + 1) * TT]


def _prep_shared(inp):
    f32 = np.float32
    r = {}
    rs = lambda a: np.ascontiguousarray(a)
    A = np.asarray(inp["L0_attn_A_ssm"], f32)
    B = np.asarray(inp["L0_attn_B_ssm"], f32)
    C = np.asarray(inp["L0_attn_C_ssm"], f32)
    resh = lambda M: rs(M.reshape(G, 128, NS).transpose(1, 0, 2)
                        .reshape(128, GN))
    r["A_r"], r["B_r"], r["C_r"] = resh(A), resh(B), resh(C)
    vec8 = lambda v: rs(np.asarray(v, f32).reshape(G, 128).T)
    r["s1_r"] = vec8(inp["L0_attn_scale1"])
    r["s2_r"] = vec8(inp["L0_attn_scale2"])
    r["D_r"] = vec8(inp["L0_attn_D_ssm"])
    r["gs_r"] = vec8(inp["L0_ffn_g_shared"])
    r["gr_r"] = vec8(inp["L0_ffn_g_routed"])
    r["b2a_r"] = vec8(inp["L0_attn_b2"])
    r["b1a_r"] = rs(np.asarray(inp["L0_attn_b1"], f32).reshape(NH, 128).T)
    r["b1s_r"] = rs(np.asarray(inp["L0_ffn_b1_s"], f32)
                    .reshape(NES, NH, 128).transpose(2, 0, 1)
                    .reshape(128, NES * NH))
    r["b1r_r"] = rs(np.asarray(inp["L0_ffn_b1_r"], f32)
                    .reshape(NER, NH, 128).transpose(2, 0, 1)
                    .reshape(128, NER * NH))
    r["b2r"] = np.asarray(inp["L0_ffn_b2_r"], f32)
    r["b2s0"] = rs(np.asarray(inp["L0_ffn_b2_s"], f32)[0:1])
    r["b2s1"] = rs(np.asarray(inp["L0_ffn_b2_s"], f32)[1:2])
    r["cent_r"] = rs(np.asarray(inp["L0_ffn_centroids"], f32)
                     .reshape(G, 128, NER).transpose(1, 0, 2)
                     .reshape(128, G * NER))
    r["w1"] = np.asarray(inp["L0_attn_w1"], f32)
    r["w2"] = np.asarray(inp["L0_attn_w2"], f32)
    r["W1s"] = np.asarray(inp["L0_ffn_W1_s"], f32)
    r["W2s"] = np.asarray(inp["L0_ffn_W2_s"], f32)
    r["W1r"] = np.asarray(inp["L0_ffn_W1_r"], f32)
    r["W2r"] = np.asarray(inp["L0_ffn_W2_r"], f32)
    r["IDENT"] = np.eye(128, dtype=f32)
    r["ONESBF"] = np.ones((128, 128), dtype=ml_dtypes.bfloat16)
    import math as _m
    dt = np.exp(np.linspace(_m.log(0.001), _m.log(0.1), NS)).astype(f32)
    r["DTREP"] = rs(np.broadcast_to(np.tile(dt, G)[None, :], (128, GN)))
    r["DTRI"] = rs(np.broadcast_to(np.tile((1.0 / dt).astype(f32), G)[None, :],
                                   (128, GN)))
    return r



def kernel(**inputs):
    TC = 2048
    NT = TC // TT
    x = np.asarray(inputs["L0_attn_x"], np.float32)
    shared = _prep_shared(inputs)
    rj = np.ascontiguousarray(np.broadcast_to(
        (TC - 1.0 - np.arange(TC, dtype=np.float32))[None, :], (128, TC)))
    zeros_fm = np.zeros((D, TC), np.float32)

    nc = build_program(NT=NT, NTP=NT)
    in_maps = []
    for c in range(NCORES):
        bb, hf = c // 2, c % 2
        m = dict(shared)
        m["x"] = np.ascontiguousarray(
            x[bb, hf * TC:(hf + 1) * TC, :].T)
        m["xp"] = (np.ascontiguousarray(x[bb, 0:TC, :].T)
                   if hf else zeros_fm)
        m["RJREV"] = rj
        wmtc = np.zeros((16, 128), np.float32)
        wmtc[8:10] = 1.0
        m["WMTC"] = wmtc
        in_maps.append(m)
    res = run_bass_kernel_spmd(nc, in_maps, core_ids=list(range(NCORES)))
    outs = [res.results[c]["out"] for c in range(NCORES)]
    return np.concatenate(outs, axis=0).reshape(BATCH, SEQ, D)
